# version 32
# baseline (speedup 1.0000x reference)
import contextlib
import numpy as np

import concourse.bass as bass
import concourse.mybir as mybir
from concourse.bass_utils import run_bass_kernel_spmd

F32 = mybir.dt.float32
BF16 = mybir.dt.bfloat16
Alu = mybir.AluOpType
Act = mybir.ActivationFunctionType
AX = mybir.AxisListType.X

NCORES = 8
P = 128
C = 8
TPC = 256
R = 2048
NT = 16
E = 1024
I = 4096
NSUP = 4
EPS = 1e-5
MAGIC = 12582912.0
W_COUNT = 4096.0 * 1024.0

OFF_G = 0
OFF_U = 512 * 1024
OFF_D = 1024 * 1024
SHARD = 3 * 512 * 1024


class Sem:
    def __init__(self, handle):
        self.h = handle
        self.n = 0


class Walk:
    def __init__(self):
        self.ops = {k: [] for k in ("sync", "tensor", "vector", "scalar", "gpsimd")}

    def emit(self, eng, fn, waits=(), inc=None, inck=1):
        if inc is not None:
            inc.n += inck
        self.ops[eng].append((fn, tuple(waits), inc, inck))

    def run(self, eng_name, eng):
        for fn, waits, inc, inck in self.ops[eng_name]:
            for s, v in waits:
                eng.wait_ge(s.h, v)
            ins = fn(eng)
            if inc is not None:
                ins.then_inc(inc.h, inck)


def build_nc():
    nc = bass.Bass("TRN2", target_bir_lowering=False, debug=False, detect_race_conditions=False)

    x_in = nc.dram_tensor("x", [R, E], F32, kind="ExternalInput").ap()
    wq_in = nc.dram_tensor("wq", [64, 64], F32, kind="ExternalInput").ap()
    wk_in = nc.dram_tensor("wk", [64, 64], F32, kind="ExternalInput").ap()
    wv_in = nc.dram_tensor("wv", [64, 64], F32, kind="ExternalInput").ap()
    wg_in = nc.dram_tensor("wg_sh", [512, E], F32, kind="ExternalInput").ap()
    wu_in = nc.dram_tensor("wu_sh", [512, E], F32, kind="ExternalInput").ap()
    wd_in = nc.dram_tensor("wd_sh", [128, I], F32, kind="ExternalInput").ap()
    y_out = nc.dram_tensor("y", [R, E], F32, kind="ExternalOutput").ap()

    cc_ar_in = nc.dram_tensor("cc_ar_in", [67, 128], F32)
    cc_ar_out = nc.dram_tensor("cc_ar_out", [67, 128], F32)
    cc_ag_in = nc.dram_tensor("cc_ag_in", [SHARD], BF16)
    cc_ag_out = nc.dram_tensor("cc_ag_out", [NCORES * SHARD], BF16, addr_space="Shared")
    xprime = nc.dram_tensor("xprime", [R, E], F32)
    vsum_dram = nc.dram_tensor("vsum_dram", [C, E], F32)
    wd_repack = nc.dram_tensor("wd_repack", [I, E], BF16)

    with contextlib.ExitStack() as st:
        def sb(name, shape, dtype=F32):
            return st.enter_context(nc.sbuf_tensor(name, shape, dtype))

        def psm(name, shape, dtype=F32):
            return st.enter_context(nc.psum_tensor(name, shape, dtype))

        identf = sb("identf", [P, P], F32)
        identb = sb("identb", [P, P], BF16)
        onesf1 = sb("onesf1", [1, P], F32)
        ones1b = sb("ones1b", [P, 1], BF16)
        cst = sb("cst", [P, 8], F32)
        wqkv = sb("wqkv", [64, 192], F32)
        wqkv_b = sb("wqkv_b", [64, 192], BF16)
        wqkv_t = sb("wqkv_t", [64, 192], BF16)
        BW = sb("BW", [P, 3, P], BF16)
        kconsts = sb("kconsts", [P, 8], F32)
        scal3 = sb("scal3", [3, P], F32)
        srow = sb("srow", [1, 8], F32)
        absn = sb("absn", [P, 12], F32)
        st_sum = sb("st_sum", [P, NT], F32)
        st_ssq = sb("st_ssq", [P, NT], F32)
        st_mean = sb("st_mean", [P, NT], F32)
        st_var = sb("st_var", [P, NT], F32)
        st_rstd = sb("st_rstd", [P, NT], F32)
        st_lnb = sb("st_lnb", [P, NT], F32)
        st_s = sb("st_s", [P, NT * 16], F32)
        st_e = sb("st_e", [P, NT * 16], F32)
        st_p = sb("st_p", [P, NT * 16], F32)
        sm_mx = sb("sm_mx", [P, 32], F32)
        sm_sum = sb("sm_sum", [P, 32], F32)
        sm_r = sb("sm_r", [P, 32], F32)
        se2 = sb("se2", [P, NT * 16], F32)
        xsum_sb = sb("xsum_sb", [C, E], F32)
        xsum_all = sb("xsum_all", [C, E], F32)
        xsumT = sb("xsumT", [P, 64], BF16)
        vsum_sb = sb("vsum_sb", [C, E], F32)
        ind3 = sb("ind3", [P, 8, 8], BF16)
        st2_sum = sb("st2_sum", [P, 4], F32)
        st2_ssq = sb("st2_ssq", [P, 4], F32)
        st2_a = sb("st2_a", [P, 4], F32)
        st2_b = sb("st2_b", [P, 4], F32)
        st2_rstd = sb("st2_rstd", [P, 4], F32)
        st2_lnb = sb("st2_lnb", [P, 4], F32)
        st_mx2 = sb("st_mx2", [P, 4], F32)
        st_c = sb("st_c", [P, 4], F32)
        st_rc = sb("st_rc", [P, 4], F32)
        st_s127 = sb("st_s127", [P, 4], F32)
        st_c3a = sb("st_c3a", [P, 4], F32)
        st_F = sb("st_F", [P, 4], F32)
        st_mxh = sb("st_mxh", [P, 4], F32)
        st_m1 = sb("st_m1", [P, 4], F32)
        st_m2 = sb("st_m2", [P, 4], F32)
        st_rc2 = sb("st_rc2", [P, 4], F32)
        st_s0 = sb("st_s0", [P, 4], F32)
        st_sint = sb("st_sint", [P, 4], F32)
        st_fd = sb("st_fd", [P, 4], F32)
        sseg_row = sb("sseg_row", [1, 512], F32)
        mxh = sb("mxh", [P, 512], F32)
        s_b = sb("s_b", [P, 512], F32)
        sepb = sb("sepb", [P, 512], F32)

        AW = 45232
        arena = sb("arena", [P, AW], F32)

        class Alloc:
            def __init__(self):
                self.off = 0

            def f32(self, words):
                o = self.off
                assert o + words <= AW, f"arena overflow {o}+{words}"
                self.off = o + words
                return arena[:, o:o + words]

            def bf16(self, elems):
                assert elems % 2 == 0
                return self.f32(elems // 2).bitcast(BF16)

        aa = Alloc()
        xr = [aa.f32(E) for _ in range(NT)]
        wshg_f = aa.f32(4 * E)
        wshg = wshg_f.rearrange("p (t i) -> p t i", i=E)
        wshu_f = aa.f32(4 * E)
        wshu = wshu_f.rearrange("p (t i) -> p t i", i=E)
        wshd = aa.f32(I)
        qshg = aa.bf16(4 * E).rearrange("p (t i) -> p t i", i=E)
        qshu = aa.bf16(4 * E).rearrange("p (t i) -> p t i", i=E)
        qshd = aa.bf16(I)
        qwgT = aa.bf16(8 * 512).rearrange("p (m o) -> p m o", o=512)
        qwuT = aa.bf16(8 * 512).rearrange("p (m o) -> p m o", o=512)
        qwdT = aa.bf16(32 * 128).rearrange("p (m o) -> p m o", o=128)
        scrS = aa.f32(E)
        x1buf = [scrS[:, 0:512].bitcast(BF16), scrS[:, 512:1024].bitcast(BF16)]
        x1T = [aa.bf16(E) for _ in range(2)]
        q_sb = aa.f32(E)
        scrV = aa.f32(E)
        qk_scr = scrV
        assert aa.off <= AW, aa.off

        fa = Alloc()
        h_raw = fa.f32(32 * 512)
        h_sb = h_raw.rearrange("p (n o) -> p n o", o=512)
        xpraw = h_raw[:, 0:4096]
        xp_buf = xpraw.rearrange("p (t i) -> p t i", i=E)
        qh_raw = fa.f32(8 * 256)
        qh_sb = qh_raw.bitcast(BF16).rearrange("p (n o) -> p n o", o=512)
        xst_raw = fa.f32(4096)
        xstage = [xst_raw[:, 0:2048].bitcast(BF16).rearrange("p (m o) -> p m o", o=128),
                  xst_raw[:, 2048:4096].bitcast(BF16).rearrange("p (m o) -> p m o", o=128)]
        qxT = fa.bf16(8 * 512).rearrange("p (m o) -> p m o", o=512)
        wg_ring = [fa.bf16(8 * 256).rearrange("p (m o) -> p m o", o=256) for _ in range(3)]
        wu_ring = [fa.bf16(8 * 256).rearrange("p (m o) -> p m o", o=256) for _ in range(3)]
        wd_ring = [fa.bf16(2 * 1024).rearrange("p (m o) -> p m o", o=1024) for _ in range(3)]
        scr_x2 = fa.f32(E)
        x2b = fa.f32(E)
        scrV2 = fa.f32(E)
        rgraw = fa.f32(1024)
        rg_sb = [rgraw[:, 0:512], rgraw[:, 512:1024]]
        A_sb = fa.f32(512)
        qh0raw = fa.f32(1024)
        x2c = rgraw
        x2d = qh0raw
        qh0 = [qh0raw[:, 0:512], qh0raw[:, 512:1024]]
        scrS3 = fa.f32(512)
        out_sb = [fa.f32(E) for _ in range(2)]
        out_sb.append(h_raw[:, 6144:7168])
        out_sb.append(h_raw[:, 7168:8192])
        res_ring = [fa.f32(E) for _ in range(2)]
        x2s = [scr_x2, x2b, x2c, x2d]
        qxs = [s[:, 0:512].bitcast(BF16) for s in x2s]
        assert fa.off <= AW, fa.off

        def qx_slot(jj):
            return qxs[jj % 4]

        psT = psm("psT", [P, 1024], F32)
        psA = psm("psA", [P, 1024], F32)
        psB = psm("psB", [P, 1024], F32)
        psC = psm("psC", [P, 1024], F32)

        names = ("d_x", "d_w", "d_cc", "d_cb", "d_xp", "d_xl", "d_ws", "d_rs", "d_y", "d_v",
                 "d_b0", "d_b1", "d_g0", "d_g1", "d_d0", "d_d1", "d_d2",
                 "c_t", "c_v", "c_s", "c_g", "s_cc")
        S = {k: Sem(st.enter_context(nc.semaphore(k))) for k in names}
        d_x, d_w, d_cc, d_cb, d_xp, d_xl, d_ws, d_rs, d_y, d_v = (S[k] for k in
            ("d_x", "d_w", "d_cc", "d_cb", "d_xp", "d_xl", "d_ws", "d_rs", "d_y", "d_v"))
        d_b = [S["d_b0"], S["d_b1"]]
        d_g = [S["d_g0"], S["d_g1"]]
        d_d = [S["d_d0"], S["d_d1"], S["d_d2"]]
        c_t, c_v, c_s, c_g, s_cc = (S[k] for k in ("c_t", "c_v", "c_s", "c_g", "s_cc"))

        W = Walk()
        em = W.emit

        def SEP(v):
            return v.tensor_copy(sepb[:, 0:512], mxh[:, 0:512])

        SYM = {}

        def resolve_all():
            for eng in W.ops:
                out = []
                for fn, waits, inc, inck in W.ops[eng]:
                    nw = []
                    for s, v in waits:
                        if isinstance(s, str):
                            sem, val = SYM[(s,) + (v if isinstance(v, tuple) else (v,))]
                            nw.append((sem, val))
                        else:
                            nw.append((s, v))
                    out.append((fn, tuple(nw), inc, inck))
                W.ops[eng] = out

        def mark(tag, *key):
            pass

        em("sync", lambda e: e.dma_start(out=wqkv[:, 0:64], in_=wq_in), inc=d_w, inck=16)
        em("sync", lambda e: e.dma_start(out=wqkv[:, 64:128], in_=wk_in), inc=d_w, inck=16)
        em("sync", lambda e: e.dma_start(out=wqkv[:, 128:192], in_=wv_in), inc=d_w, inck=16)
        for j in range(NT):
            em("sync", lambda e, j=j: e.dma_start(
                out=xr[j], in_=x_in[j * P:(j + 1) * P, :]), inc=d_x, inck=16)
        em("sync", lambda e: e.dma_start(
            out=wshg, in_=wg_in.rearrange("(t p) i -> p t i", p=P)), inc=d_w, inck=16)
        em("sync", lambda e: e.dma_start(
            out=wshu, in_=wu_in.rearrange("(t p) i -> p t i", p=P)), inc=d_w, inck=16)
        em("sync", lambda e: e.dma_start(out=wshd, in_=wd_in), inc=d_w, inck=16)
        DW_IN = d_w.n

        def vconsts(v):
            v.memset(sepb[:], 0.0)
            v.memset(mxh[:], 0.0)
            v.memset(identf[:], 0.0)
            v.memset(onesf1[:], 1.0)
            v.memset(ones1b[:], 1.0)
            v.memset(cst[:, 0:1], EPS)
            v.memset(cst[:, 1:2], MAGIC)
            v.memset(cst[:, 2:3], -0.5)
            v.memset(cst[:, 3:4], 1.0 / 32.0)
            v.memset(cst[:, 4:5], 1.0)
            v.memset(cst[:, 5:6], 0.0)
            v.memset(cst[:, 6:7], -MAGIC)
            v.memset(BW[:], 0.0)
            v.memset(ind3[:], 0.0)
            v.memset(scal3[:], 0.0)
            v.memset(srow[:], 0.0)
            SEP(v)
            for _c in range(8):
                v.memset(ind3[:, _c, _c:_c + 1], 1.0)
            return SEP(v)
        em("vector", vconsts, inc=c_v)
        CV_CONST = c_v.n

        def gconsts(g):
            g.wait_ge(c_v.h, CV_CONST)
            return g.affine_select(out=identf[:], in_=identf[:],
                                   compare_op=Alu.not_equal, fill=1.0, base=0,
                                   pattern=[[-1, P]], channel_multiplier=1)
        em("gpsimd", gconsts, inc=c_g)
        CG = c_g.n

        em("vector", lambda e: e.tensor_copy(identb[:], identf[:]),
           waits=[(c_g, CG)])
        em("vector", lambda e: SEP(e), inc=c_v)
        CV_IDB = c_v.n
        em("vector", lambda e: e.tensor_copy(wqkv_b[:], wqkv[:]),
           waits=[(d_w, DW_IN)], inc=c_v)
        CV_WQKVB = c_v.n
        for w in range(3):
            em("tensor", lambda e, w=w: e.matmul(
                psT[0:64, 64 * w:64 * w + 64], wqkv_b[:, 64 * w:64 * w + 64],
                identb[0:64, 0:64], start=True, stop=True),
               waits=[(c_v, CV_WQKVB)] if w == 0 else [],
               inc=c_t if w == 2 else None)
        CT_WQT = c_t.n
        em("vector", lambda e: e.tensor_copy(wqkv_t[:, :], psT[0:64, 0:192]),
           waits=[(c_t, CT_WQT)])
        em("vector", lambda e: SEP(e), inc=c_v)
        CV_WQT = c_v.n
        for w in range(3):
            em("sync", lambda e, w=w: e.dma_start(
                out=BW[0:64, w, 0:64], in_=wqkv_t[:, 64 * w:64 * w + 64]),
               waits=[(c_v, CV_WQT)] if w == 0 else [], inc=d_w, inck=16)
            em("sync", lambda e, w=w: e.dma_start(
                out=BW[64:128, w, 64:128], in_=wqkv_t[:, 64 * w:64 * w + 64]),
               inc=d_w, inck=16)
        DW_BW = d_w.n

        CV_LNB1G = [0] * 4
        for g4 in range(4):
            sl = slice(4 * g4, 4 * g4 + 4)
            for j in range(4 * g4, 4 * g4 + 4):
                em("vector", lambda e, j=j: e.scalar_tensor_tensor(
                    scrV[:], xr[j], 1.0, xr[j], Alu.mult, Alu.mult,
                    accum_out=st_ssq[:, j:j + 1]),
                   waits=[(d_x, 16 * (4 * g4 + 4))] if j % 4 == 0 else [])
                em("vector", lambda e, j=j: e.tensor_scalar(
                    scrV[:], xr[j], 1.0, 0.0, Alu.mult, Alu.add,
                    accum_out=st_sum[:, j:j + 1]))

            def ln1math(v, sl=sl):
                SEP(v)
                v.tensor_scalar(st_mean[:, sl], st_sum[:, sl], 1.0 / E, None, Alu.mult)
                v.tensor_mul(st_lnb[:, sl], st_sum[:, sl], st_sum[:, sl])
                v.tensor_scalar(st_var[:, sl], st_ssq[:, sl], 1.0 / E, None, Alu.mult)
                SEP(v)
                v.tensor_scalar(st_lnb[:, sl], st_lnb[:, sl], 1.0 / (E * E), None,
                                Alu.mult)
                SEP(v)
                v.tensor_sub(st_var[:, sl], st_var[:, sl], st_lnb[:, sl])
                return SEP(v)
            em("vector", ln1math, inc=c_v)
            CV_LN1V = c_v.n

            def ln1rstd(s, sl=sl):
                s.activation(st_lnb[:, sl], st_var[:, sl], Act.Ln, bias=cst[:, 0:1],
                             scale=cst[:, 4:5])
                return s.activation(st_rstd[:, sl], st_lnb[:, sl], Act.Exp,
                                    bias=cst[:, 5:6], scale=cst[:, 2:3])
            em("scalar", ln1rstd, waits=[(c_v, CV_LN1V)], inc=c_s)
            CS_RSTD1 = c_s.n
            em("vector", lambda e, sl=sl: e.scalar_tensor_tensor(
                st_lnb[:, sl], st_mean[:, sl], -1.0, st_rstd[:, sl],
                Alu.mult, Alu.mult),
               waits=[(c_s, CS_RSTD1)])
            em("vector", lambda e: SEP(e), inc=c_v)
            CV_LNB1G[g4] = c_v.n

        def wabs(v):
            v.reduce_sum(absn[:, 0:1], wshg_f, AX, apply_absolute_value=True)
            v.reduce_sum(absn[:, 1:2], wshu_f, AX, apply_absolute_value=True)
            v.reduce_sum(absn[:, 2:3], wshd[:, :], AX, apply_absolute_value=True)
            return SEP(v)
        em("vector", wabs, waits=[(d_w, DW_IN)], inc=c_v)
        CV_WABS = c_v.n

        CS_X1 = [0] * NT
        CT_TT = [0] * NT
        CT_QK = [0] * NT
        CV_EVX = [0] * NT
        CV_QK = [0] * NT

        for j in range(NT):
            waits = [(c_v, CV_LNB1G[j // 4])]
            if j >= 2:
                waits.append(("CT_QK", j - 2))
            em("scalar", lambda e, j=j: e.activation(
                x1buf[j % 2][:], xr[j], Act.Identity,
                scale=st_rstd[:, j:j + 1], bias=st_lnb[:, j:j + 1]),
               waits=waits, inc=c_s)
            CS_X1[j] = c_s.n
            SYM[("CS_X1", j)] = (c_s, c_s.n)

        for j in range(NT):
            waits = [("CS_X1", j), (c_v, CV_IDB)]
            if j >= 1:
                waits.append(("CV_EVX", j - 1))
            for m in range(8):
                em("tensor", lambda e, j=j, m=m: e.matmul(
                    psT[:, 128 * m:128 * m + 128], x1buf[j % 2][:, 128 * m:128 * m + 128],
                    identb[:], start=True, stop=True),
                   waits=waits if m == 0 else [], inc=c_t if m == 7 else None)
            CT_TT[j] = c_t.n
            SYM[("CT_TT", j)] = (c_t, c_t.n)

            w_q = [(d_w, DW_BW), ("CV_EVX", j)]
            if j >= 1:
                w_q.append(("CV_QK", j - 1))
            for m in range(8):
                em("tensor", lambda e, j=j, m=m: e.matmul(
                    psA[:, 128 * m:128 * m + 128], x1T[j % 2][:, 128 * m:128 * m + 128],
                    BW[:, 0, :], start=True, stop=True),
                   waits=w_q if m == 0 else [])
            for m in range(8):
                em("tensor", lambda e, j=j, m=m: e.matmul(
                    psB[:, 128 * m:128 * m + 128], x1T[j % 2][:, 128 * m:128 * m + 128],
                    BW[:, 1, :], start=True, stop=True))
            c = j // 2
            for half in range(2):
                em("tensor", lambda e, j=j, c=c, half=half: e.matmul(
                    psC[0:8, 512 * half:512 * half + 512], ind3[:, c, :],
                    x1buf[j % 2][:, 512 * half:512 * half + 512],
                    start=(j == 0), stop=(j == NT - 1)),
                   inc=c_t if half == 1 else None)
            CT_QK[j] = c_t.n
            SYM[("CT_QK", j)] = (c_t, c_t.n)

            em("vector", lambda e, j=j: e.tensor_copy(x1T[j % 2][:], psT[:, :]),
               waits=[(c_t, CT_TT[j])], inc=c_v)
            CV_EVX[j] = c_v.n
            SYM[("CV_EVX", j)] = (c_v, c_v.n)
            em("vector", lambda e, j=j: e.tensor_copy(q_sb[:], psA[:, :]),
               waits=[(c_t, CT_QK[j])], inc=c_v)
            em("vector", lambda e, j=j: e.tensor_mul(qk_scr[:], q_sb[:], psB[:, :]))
            em("vector", lambda e, j=j: e.reduce_sum(
                st_s[:, 16 * j:16 * j + 16].rearrange("p (a b) -> p a b", b=1),
                qk_scr[:].rearrange("p (a b) -> p a b", b=64), AX), inc=c_v)
            CV_QK[j] = c_v.n
            SYM[("CV_QK", j)] = (c_v, c_v.n)

        em("tensor", lambda e: e.matmul(psT[0:3, 0:128], absn[:, 0:3], identf[:],
                                        start=True, stop=True),
           waits=[(c_v, CV_WABS), (c_v, CV_EVX[NT - 1])], inc=c_t)
        CT_WPT = c_t.n
        em("vector", lambda e: e.reduce_sum(scal3[0:3, 0:1], psT[0:3, 0:128], AX),
           waits=[(c_t, CT_WPT)], inc=c_v)
        em("vector", lambda e: e.tensor_copy(xsum_sb[:, :], psC[0:8, :]),
           waits=[(c_t, CT_QK[NT - 1])])
        em("vector", lambda e: SEP(e), inc=c_v)
        CV_CCRDY = c_v.n
        em("sync", lambda e: e.dma_start(
            out=cc_ar_in.ap()[0:64, :].rearrange("(c k) f -> c (k f)", k=8),
            in_=xsum_sb[:, :]), waits=[(c_v, CV_CCRDY)], inc=d_cc, inck=16)
        em("sync", lambda e: e.dma_start(out=cc_ar_in.ap()[64:67, :], in_=scal3[:, :]),
           inc=d_cc, inck=16)
        DCC_IN = d_cc.n
        em("gpsimd", lambda e: e.collective_compute(
            "AllReduce", Alu.add, replica_groups=[list(range(NCORES))],
            ins=[cc_ar_in.ap()], outs=[cc_ar_out.ap()]),
           waits=[(d_cc, DCC_IN)], inc=s_cc)
        SCC_AR = s_cc.n
        em("gpsimd", lambda e: e.dma_start(
            out=xsum_all[:, :],
            in_=cc_ar_out.ap()[0:64, :].rearrange("(c k) f -> c (k f)", k=8)),
           waits=[(s_cc, SCC_AR)], inc=d_cb, inck=16)
        em("gpsimd", lambda e: e.dma_start(out=scal3[0:3, 0:128],
                                           in_=cc_ar_out.ap()[64:67, 0:128]),
           inc=d_cb, inck=16)
        DCC_BACK = d_cb.n

        em("tensor", lambda e: e.matmul(psT[0:1, 0:3], scal3[0:3, 0:1], identf[0:3, 0:3],
                                        start=True, stop=True),
           waits=[(d_cb, DCC_BACK)], inc=c_t)
        CT_S3 = c_t.n

        def smath(v):
            v.tensor_copy(srow[:, 0:3], psT[0:1, 0:3])
            SEP(v)
            v.tensor_scalar(srow[:, 0:3], srow[:, 0:3], 1.0 / W_COUNT, EPS, Alu.mult, Alu.max)
            SEP(v)
            v.tensor_mul(srow[:, 3:4], srow[:, 0:1], srow[:, 0:1])
            v.tensor_scalar(srow[:, 4:5], srow[:, 2:3], 1.0 / 127.0, None, Alu.mult)
            v.reciprocal(srow[:, 5:8], srow[:, 0:3])
            SEP(v)
            v.tensor_mul(srow[:, 3:4], srow[:, 3:4], srow[:, 1:2])
            SEP(v)
            v.tensor_scalar(srow[:, 3:4], srow[:, 3:4], 1.0 / 2048383.0, None, Alu.mult)
            v.tensor_copy(srow[:, 0:1], srow[:, 5:6])
            v.tensor_copy(srow[:, 1:2], srow[:, 6:7])
            v.tensor_copy(srow[:, 2:3], srow[:, 7:8])
            return SEP(v)
        em("vector", smath, waits=[(c_t, CT_S3)], inc=c_v)
        CV_SMATH = c_v.n
        em("tensor", lambda e: e.matmul(psT[0:P, 0:8], onesf1[0:1, :], srow[0:1, 0:8],
                                        start=True, stop=True),
           waits=[(c_v, CV_SMATH)], inc=c_t)
        CT_KC = c_t.n
        em("vector", lambda e: e.tensor_copy(kconsts[:, :], psT[0:P, 0:8]),
           waits=[(c_t, CT_KC)])
        em("vector", lambda e: SEP(e), inc=c_v)
        CV_KC = c_v.n

        qv_done = {}

        def quant(v, wsrc, qdst, col):
            v.tensor_scalar(scrV[:, 0:wsrc.shape[-1]], wsrc,
                            kconsts[:, col:col + 1], MAGIC, Alu.mult, Alu.add)
            v.tensor_scalar(scrV[:, 0:wsrc.shape[-1]], scrV[:, 0:wsrc.shape[-1]],
                            MAGIC, -1.0, Alu.subtract, Alu.max)
            return v.tensor_scalar(qdst, scrV[:, 0:wsrc.shape[-1]], 1.0, None, Alu.min)
        for t in range(4):
            em("vector", lambda e, t=t: quant(e, wshg[:, t, :], qshg[:, t, :], 0),
               waits=[(c_v, CV_KC)] if t == 0 else [], inc=c_v)
            qv_done[("g", t)] = c_v.n
        for t in range(4):
            em("vector", lambda e, t=t: quant(e, wshu[:, t, :], qshu[:, t, :], 1), inc=c_v)
            qv_done[("u", t)] = c_v.n
        for t in range(4):
            em("vector", lambda e, t=t: quant(
                e, wshd[:, 1024 * t:1024 * (t + 1)], qshd[:, 1024 * t:1024 * (t + 1)], 2),
               inc=c_v)
            qv_done[("d", t)] = c_v.n

        CV_WEV = []

        def wtr_batch(which, blocks, evac):
            for i, (lhs, slot) in enumerate(blocks):
                wts = []
                if i == 0:
                    wts = [(c_v, qv_done[which]),
                           (c_v, CV_WEV[-1] if CV_WEV else CV_KC)]
                em("tensor", lambda e, lhs=lhs, slot=slot: e.matmul(
                    psT[:, 128 * slot:128 * slot + 128], lhs, identb[:],
                    start=True, stop=True),
                   waits=wts, inc=c_t if i == len(blocks) - 1 else None)
            ct = c_t.n
            em("vector", lambda e, evac=evac: e.tensor_copy(
                evac, psT[:, :].rearrange("p (m o) -> p m o", o=128)),
               waits=[(c_t, ct)], inc=c_v)
            CV_WEV.append(c_v.n)

        for t in range(4):
            wtr_batch(("g", t), [(qshg[:, t, 128 * m:128 * m + 128], m) for m in range(8)],
                      qwgT[:, :, 128 * t:128 * t + 128])
        for t in range(4):
            wtr_batch(("u", t), [(qshu[:, t, 128 * m:128 * m + 128], m) for m in range(8)],
                      qwuT[:, :, 128 * t:128 * t + 128])
        for t in range(4):
            wtr_batch(("d", t),
                      [(qshd[:, 128 * (8 * t + mm):128 * (8 * t + mm) + 128], mm)
                       for mm in range(8)],
                      qwdT[:, 8 * t:8 * t + 8, :])
        em("vector", lambda e: SEP(e), inc=c_v)
        CV_WEV.append(c_v.n)
        CV_WTR_LAST = CV_WEV[-1]

        bmask = wshu_f[0:16, 0:E]
        indf = wshg_f[0:8, 0:128]

        def bm_init(v):
            v.memset(bmask[:, :], 1.0)
            return v.memset(indf[:, :], 1.0)
        em("vector", bm_init, waits=[(c_v, CV_WTR_LAST)], inc=c_v)
        CV_BM1 = c_v.n

        def bmask_sel(g):
            g.affine_select(out=bmask[:, :], in_=bmask[:, :],
                            compare_op=Alu.is_ge, fill=0.0, base=0,
                            pattern=[[1, E]], channel_multiplier=-64)
            g.affine_select(out=bmask[:, :], in_=bmask[:, :],
                            compare_op=Alu.is_ge, fill=0.0, base=63,
                            pattern=[[-1, E]], channel_multiplier=64)
            g.affine_select(out=indf[:, :], in_=indf[:, :],
                            compare_op=Alu.is_ge, fill=0.0, base=0,
                            pattern=[[1, 128]], channel_multiplier=-16)
            return g.affine_select(out=indf[:, :], in_=indf[:, :],
                                   compare_op=Alu.is_ge, fill=0.0, base=15,
                                   pattern=[[-1, 128]], channel_multiplier=16)
        em("gpsimd", bmask_sel, waits=[(c_v, CV_BM1)], inc=c_g)
        CG_BM = c_g.n

        ag = cc_ag_in.ap()
        em("sync", lambda e: e.dma_start(
            out=ag[OFF_G:OFF_G + 512 * 1024].rearrange("(m p o) -> p m o", p=P, o=512),
            in_=qwgT), waits=[(c_v, CV_WTR_LAST)], inc=d_cc, inck=16)
        em("sync", lambda e: e.dma_start(
            out=ag[OFF_U:OFF_U + 512 * 1024].rearrange("(m p o) -> p m o", p=P, o=512),
            in_=qwuT), inc=d_cc, inck=16)
        em("sync", lambda e: e.dma_start(
            out=ag[OFF_D:OFF_D + 512 * 1024].rearrange("(m p o) -> p m o", p=P, o=128),
            in_=qwdT), inc=d_cc, inck=16)
        DCC_AG = d_cc.n
        em("gpsimd", lambda e: e.collective_compute(
            "AllGather", Alu.bypass, replica_groups=[list(range(NCORES))],
            ins=[cc_ag_in.ap()], outs=[cc_ag_out.ap()]),
           waits=[(d_cc, DCC_AG)], inc=s_cc)
        SCC_AG = s_cc.n

        def softmax1(v):
            SEP(v)
            for b in range(2):
                for q4 in range(4):
                    j0, j1 = 4 * q4 + b, 4 * q4 + 2 + b
                    v.tensor_tensor(se2[:, 16 * (2 * q4 + b):16 * (2 * q4 + b) + 16],
                                    st_s[:, 16 * j0:16 * j0 + 16],
                                    st_s[:, 16 * j1:16 * j1 + 16], Alu.max)
            for b in range(2):
                for q2 in range(2):
                    v.tensor_tensor(
                        se2[:, 128 + 16 * (2 * q2 + b):128 + 16 * (2 * q2 + b) + 16],
                        se2[:, 16 * (4 * q2 + b):16 * (4 * q2 + b) + 16],
                        se2[:, 16 * (4 * q2 + 2 + b):16 * (4 * q2 + 2 + b) + 16],
                        Alu.max)
            SEP(v)
            for b in range(2):
                v.tensor_tensor(sm_mx[:, 16 * b:16 * b + 16],
                                se2[:, 128 + 16 * b:128 + 16 * b + 16],
                                se2[:, 128 + 16 * (2 + b):128 + 16 * (2 + b) + 16],
                                Alu.max)
            SEP(v)
            for j in range(NT):
                b = j % 2
                v.tensor_sub(st_e[:, 16 * j:16 * j + 16],
                             st_s[:, 16 * j:16 * j + 16], sm_mx[:, 16 * b:16 * b + 16])
            return SEP(v)
        em("vector", softmax1, waits=[(c_v, CV_QK[NT - 1])], inc=c_v)
        CV_SM1 = c_v.n
        em("scalar", lambda e: e.activation(se2[:], st_e[:], Act.Exp,
                                            bias=cst[:, 5:6], scale=cst[:, 3:4]),
           waits=[(c_v, CV_SM1)], inc=c_s)
        CS_EXP = c_s.n

        def softmax2(v):
            for b in range(2):
                for q4 in range(4):
                    j0, j1 = 4 * q4 + b, 4 * q4 + 2 + b
                    v.tensor_add(st_e[:, 16 * (2 * q4 + b):16 * (2 * q4 + b) + 16],
                                 se2[:, 16 * j0:16 * j0 + 16],
                                 se2[:, 16 * j1:16 * j1 + 16])
            for b in range(2):
                for q2 in range(2):
                    v.tensor_add(
                        st_e[:, 128 + 16 * (2 * q2 + b):128 + 16 * (2 * q2 + b) + 16],
                        st_e[:, 16 * (4 * q2 + b):16 * (4 * q2 + b) + 16],
                        st_e[:, 16 * (4 * q2 + 2 + b):16 * (4 * q2 + 2 + b) + 16])
            SEP(v)
            for b in range(2):
                v.tensor_add(sm_sum[:, 16 * b:16 * b + 16],
                             st_e[:, 128 + 16 * b:128 + 16 * b + 16],
                             st_e[:, 128 + 16 * (2 + b):128 + 16 * (2 + b) + 16])
            SEP(v)
            v.reciprocal(sm_r[:, 0:32], sm_sum[:, 0:32])
            SEP(v)
            for j in range(NT):
                b = j % 2
                v.tensor_mul(st_p[:, 16 * j:16 * j + 16], se2[:, 16 * j:16 * j + 16],
                             sm_r[:, 16 * b:16 * b + 16])
            return SEP(v)
        em("vector", softmax2, waits=[(c_s, CS_EXP)], inc=c_v)
        CV_P = c_v.n

        for m in range(8):
            em("tensor", lambda e, m=m: e.matmul(
                psT[0:P, 8 * m:8 * m + 8], xsum_all[:, 128 * m:128 * m + 128],
                identf[0:8, 0:8], start=True, stop=True),
               waits=[(d_cb, DCC_BACK), (c_v, CV_WTR_LAST)] if m == 0 else [],
               inc=c_t if m == 7 else None)
        CT_XST = c_t.n
        em("vector", lambda e: e.tensor_copy(xsumT[:, :], psT[:, 0:64]),
           waits=[(c_t, CT_XST)])
        em("vector", lambda e: SEP(e), inc=c_v)
        CV_XST = c_v.n
        for m in range(8):
            em("tensor", lambda e, m=m: e.matmul(
                psA[0:8, 128 * m:128 * m + 128], xsumT[:, 8 * m:8 * m + 8],
                BW[:, 2, :], start=True, stop=True),
               waits=[(c_v, CV_XST)] if m == 0 else [], inc=c_t if m == 7 else None)
        CT_VS = c_t.n
        em("vector", lambda e: e.tensor_copy(vsum_sb[:, :], psA[0:8, :]),
           waits=[(c_t, CT_VS)])
        em("vector", lambda e: SEP(e), inc=c_v)
        CV_VS = c_v.n
        CV_ADD = [0] * NT
        DXP = [0] * NT

        def emit_ln2_stats(jj_waits):
            for jj in range(4):
                wts = list(jj_waits[jj])
                em("vector", lambda e, jj=jj: e.scalar_tensor_tensor(
                    scrV2[:], xp_buf[:, jj, :], 1.0, xp_buf[:, jj, :],
                    Alu.mult, Alu.mult, accum_out=st2_ssq[:, jj:jj + 1]),
                   waits=wts)
                em("vector", lambda e, jj=jj: e.tensor_scalar(
                    scrV2[:], xp_buf[:, jj, :], 1.0, 0.0, Alu.mult, Alu.add,
                    accum_out=st2_sum[:, jj:jj + 1]))

            def ln2math(v):
                SEP(v)
                v.tensor_scalar(st2_a[:], st2_sum[:], 1.0 / E, None, Alu.mult)
                v.tensor_mul(st2_b[:], st2_sum[:], st2_sum[:])
                v.tensor_scalar(st2_rstd[:], st2_ssq[:], 1.0 / E, None, Alu.mult)
                SEP(v)
                v.tensor_scalar(st2_b[:], st2_b[:], 1.0 / (E * E), None, Alu.mult)
                SEP(v)
                v.tensor_sub(st2_rstd[:], st2_rstd[:], st2_b[:])
                return SEP(v)
            em("vector", ln2math, inc=c_v)
            return c_v.n

        vmat = scrV[0:16, :]
        pt_sl = [se2[0:16, 0:128], se2[0:16, 128:256]]
        CV_PTEV = {}
        CT_ATTN = {}
        CV_ADDR = {}
        for j in range(NT):
            c = j // 2
            if j % 2 == 0:
                wts = [(c_v, CV_VS)] if c == 0 else [(c_v, CV_VM)]
                if c == 0:
                    wts.append((c_g, CG_BM))
                for half in range(2):
                    em("tensor", lambda e, c=c, half=half: e.matmul(
                        psA[0:16, 512 * half:512 * half + 512],
                        indf[:, 16 * c:16 * c + 16],
                        vsum_sb[0:8, 512 * half:512 * half + 512],
                        start=True, stop=True),
                       waits=wts if half == 0 else [],
                       inc=c_t if half == 1 else None)
                ct_rep = c_t.n
                em("vector", lambda e: e.tensor_mul(
                    vmat, psA[0:16, :], bmask[:, :]),
                   waits=[(c_t, ct_rep)], inc=c_v)
                CV_VM = c_v.n
            wts = [(c_v, CV_PTEV[j - 4])] if j >= 4 else [(c_v, CV_XST)]
            if j == 0:
                wts.append((c_v, CV_P))
            em("tensor", lambda e, j=j: e.matmul(
                psT[0:16, 128 * (j % 4):128 * (j % 4) + 128],
                st_p[:, 16 * j:16 * j + 16], identf[:], start=True, stop=True),
               waits=wts, inc=c_t)
            ct_pt = c_t.n
            em("vector", lambda e, j=j: e.tensor_copy(
                pt_sl[j % 2], psT[0:16, 128 * (j % 4):128 * (j % 4) + 128]),
               waits=[(c_t, ct_pt)], inc=c_v)
            CV_PTEV[j] = c_v.n
            psX = psB if j % 2 == 0 else psC
            for half in range(2):
                wts = [(c_v, CV_PTEV[j]), (c_v, CV_VM)] if half == 0 else []
                if half == 0 and j >= 2:
                    wts.append((c_v, CV_ADDR[j - 2]))
                em("tensor", lambda e, j=j, half=half, psX=psX: e.matmul(
                    psX[:, 512 * half:512 * half + 512],
                    pt_sl[j % 2], vmat[:, 512 * half:512 * half + 512],
                    start=True, stop=True),
                   waits=wts, inc=c_t if half == 1 else None)
            CT_ATTN[j] = c_t.n
            em("vector", lambda e, j=j, psX=psX: e.tensor_add(
                xr[j][:, :], psX[:, :], xr[j][:, :]),
               waits=[(c_t, CT_ATTN[j])], inc=c_v)
            CV_ADDR[j] = c_v.n
            em("vector", lambda e: SEP(e), inc=c_v)
            CV_ADD[j] = c_v.n
            SYM[("CV_ADD", j)] = (c_v, c_v.n)
            em("sync", lambda e, j=j: e.dma_start(
                out=xprime.ap()[j * P:(j + 1) * P, :], in_=xr[j]),
               waits=[(c_v, CV_ADD[j])], inc=d_xp, inck=16)
            DXP[j] = d_xp.n
            if j == 3:
                CV_LN2_0 = emit_ln2_stats([[(d_cc, DCC_AG)], [], [], []])

        ago = cc_ag_out.ap().rearrange("(r q) -> r q", q=SHARD)

        def fetch_gu(e, ring, r, hb, off):
            src2 = ago[:, off:off + 512 * 1024].rearrange(
                "r (m p o) -> r m p o", p=P, o=512)[r, :, :, 256 * hb:256 * hb + 256]
            return e.dma_start(out=ring, in_=src2.transpose([1, 0, 2]))

        wdr = wd_repack.ap().rearrange("(m p) o -> m p o", p=P)

        def fetch_wd(e, ring, g):
            return e.dma_start(
                out=ring, in_=wdr[2 * g:2 * g + 2].transpose([1, 0, 2]))

        CT_GU = {}
        CV_OUT = {}
        CV_QXEV = {}
        FRONT_EVACS = []
        CT_DONE = [0] * NSUP
        DWS_GU = {}
        DWS_D = {}
        DRS = {}

        agd = ago[:, OFF_D:OFF_D + 512 * 1024].rearrange(
            "r (m p o) -> r m p o", p=P, o=128)

        def emit_repack():
            DWS_LD = {}
            DWS_ST = {}
            for r in range(8):
                wts = []
                if r == 0:
                    wts = [(s_cc, SCC_AG)]
                elif r >= 2:
                    wts = [(d_ws, DWS_ST[r - 2])]
                em("sync", lambda e, r=r: e.dma_start(
                    out=xstage[r % 2], in_=agd[r].transpose([1, 0, 2])),
                   waits=wts, inc=d_ws, inck=16)
                DWS_LD[r] = d_ws.n
                em("sync", lambda e, r=r: e.dma_start(
                    out=wdr[:, :, 128 * r:128 * r + 128].transpose([1, 0, 2]),
                    in_=xstage[r % 2]), waits=[(d_ws, DWS_LD[r])],
                   inc=d_ws, inck=16)
                DWS_ST[r] = d_ws.n
            SYM[("DWS_REP", 0)] = (d_ws, d_ws.n)
            return d_ws.n

        DXL_S = {}

        def front_dma(sg, pre_repack=False):
            if sg > 0:
                for t in range(4):
                    wts = [("CS_QH", (sg - 1, 11))] if t == 0 else []
                    em("sync", lambda e, sg=sg, t=t: e.dma_start(
                        out=xp_buf[:, t, :],
                        in_=xprime.ap()[512 * sg + 128 * t:512 * sg + 128 * (t + 1), :]),
                       waits=wts, inc=d_xl, inck=16)
                    DXL_S[(sg, t)] = d_xl.n

            for fi in range(16):
                r, hb = fi // 2, fi % 2
                wts = [(s_cc, SCC_AG)] if (sg == 0 and fi == 0) else []
                if sg > 0 or fi >= 3:
                    pr, pn = (sg, 2 * (fi - 3) + 1) if fi >= 3 \
                        else (sg - 1, [31, 27, 29][fi])
                    wts.append(("CT_GU", (pr, pn)))
                em("sync", lambda e, r=r, hb=hb, fi=fi: fetch_gu(
                    e, wg_ring[fi % 3], r, hb, OFF_G), waits=wts,
                   inc=d_g[fi % 2], inck=16)
                em("sync", lambda e, r=r, hb=hb, fi=fi: fetch_gu(
                    e, wu_ring[fi % 3], r, hb, OFF_U), inc=d_g[fi % 2], inck=16)
                DWS_GU[(sg, fi)] = (d_g[fi % 2], d_g[fi % 2].n)
                if pre_repack and fi == 2:
                    emit_repack()
            for g in range(16):
                wts = []
                gi = 16 * sg + g
                if sg == 1 and g == 0:
                    wts.append(("DWS_REP", 0))
                if gi >= 3:
                    pg = gi - 3
                    wts.append(("CT_DW", (pg // 16, pg % 16)))
                if sg == 0:
                    for s in range(8):
                        em("gpsimd", lambda e, g=g, s=s, gi=gi: e.dma_start(
                            out=wd_ring[gi % 3][:, :, 128 * s:128 * s + 128],
                            in_=agd[s][2 * g:2 * g + 2].transpose([1, 0, 2])),
                           waits=(wts + [(s_cc, SCC_AG)]) if s == 0 else [],
                           inc=d_d[gi % 3], inck=16)
                else:
                    em("gpsimd", lambda e, sg=sg, g=g, gi=gi: fetch_wd(
                        e, wd_ring[gi % 3], g), waits=wts, inc=d_d[gi % 3], inck=16)
                DWS_D[(sg, g)] = (d_d[gi % 3], d_d[gi % 3].n)
            for b in range(4):
                wts = [(d_xp, 16 * (4 * sg + b + 1))]
                prev = 4 * sg + b - 1
                if prev >= 1:
                    wts.append((d_y, 16 * prev))
                em("gpsimd", lambda e, sg=sg, b=b: e.dma_start(
                    out=res_ring[b % 2],
                    in_=xprime.ap()[512 * sg + 128 * b:512 * sg + 128 * (b + 1), :]),
                   waits=wts, inc=d_rs, inck=16)
                DRS[(sg, b)] = d_rs.n

        CV_LN2S = {0: None}

        def front_stats(sg):
            CV_LN2S[sg] = emit_ln2_stats(
                [[(d_xl, DXL_S[(sg, t)])] for t in range(4)])

        def front_rest(sg):
            if sg > 0 and sg not in CV_LN2S:
                front_stats(sg)
            CV_LN2 = CV_LN2_0 if sg == 0 else CV_LN2S[sg]

            def ln2rstd(s):
                s.activation(st2_b[:], st2_rstd[:], Act.Ln, bias=cst[:, 0:1],
                             scale=cst[:, 4:5])
                return s.activation(st2_rstd[:], st2_b[:], Act.Exp, bias=cst[:, 5:6],
                                    scale=cst[:, 2:3])
            em("scalar", ln2rstd, waits=[(c_v, CV_LN2)], inc=c_s)
            CS_R2 = c_s.n
            em("vector", lambda e: e.scalar_tensor_tensor(
                st2_lnb[:], st2_a[:], -1.0, st2_rstd[:], Alu.mult, Alu.mult),
               waits=[(c_s, CS_R2)])
            em("vector", lambda e: SEP(e), inc=c_v)
            CV_LNB2 = c_v.n

            def chain(v, k):
                slot = x2s[k % 4]
                v.tensor_scalar(st_c[:, k:k + 1], st_mx2[:, k:k + 1], EPS, None, Alu.max)
                SEP(v)
                v.reciprocal(st_rc[:, k:k + 1], st_c[:, k:k + 1])
                v.tensor_mul(st_c3a[:, k:k + 1], st_c[:, k:k + 1], st_c[:, k:k + 1])
                v.tensor_scalar(scrV2[:], slot, 127.0, None, Alu.mult)
                v.tensor_mul(st_s0[:, k:k + 1], st_c3a[:, k:k + 1], st_c[:, k:k + 1])
                v.tensor_scalar(scrV2[:], scrV2[:], st_rc[:, k:k + 1], MAGIC,
                                Alu.mult, Alu.add)
                v.tensor_scalar(st_F[:, k:k + 1], st_s0[:, k:k + 1],
                                kconsts[:, 3:4], None, Alu.mult)
                return v.tensor_scalar(qx_slot(k), scrV2[:], MAGIC, None, Alu.subtract)

            def emit_qxT(k):
                wts = [("CV_QX", (sg, k))]
                if k == 0 and sg > 0:
                    wts.append(("CV_OUT0", sg - 1))
                if k > 0:
                    wts.append(("CV_QXEV", (sg, k - 1)))
                for m in range(8):
                    em("tensor", lambda e, k=k, m=m: e.matmul(
                        psT[:, 128 * m:128 * m + 128],
                        qx_slot(k)[:, 128 * m:128 * m + 128],
                        identb[:], start=True, stop=True),
                       waits=wts if m == 0 else [], inc=c_t if m == 7 else None)
                SYM[("CT_QXT", sg, k)] = (c_t, c_t.n)
                ctv = c_t.n

                def do_evac():
                    em("vector", lambda e, k=k: e.tensor_copy(
                        qxT[:, :, 128 * k:128 * k + 128],
                        psT[:, :].rearrange("p (m o) -> p m o", o=128)),
                       waits=[(c_t, ctv)], inc=c_v)
                    CV_QXEV[(sg, k)] = c_v.n
                    SYM[("CV_QXEV", sg, k)] = (c_v, c_v.n)
                if sg == 0:
                    do_evac()
                else:
                    FRONT_EVACS.append(do_evac)

            for jj in range(4):
                wts = [(c_v, CV_LNB2)]
                if sg >= 1:
                    wts.append(("CT_QXT", (sg - 1, jj)))
                em("scalar", lambda e, jj=jj: e.activation(
                    x2s[jj % 4], xp_buf[:, jj, :], Act.Identity,
                    scale=st2_rstd[:, jj:jj + 1], bias=st2_lnb[:, jj:jj + 1]),
                   waits=wts, inc=c_s)
                cs_x2 = c_s.n
                em("vector", lambda e, jj=jj: e.reduce_max(
                    st_mx2[:, jj:jj + 1], x2s[jj % 4], AX, apply_absolute_value=True),
                   waits=[(c_s, cs_x2)])
                if jj >= 1:
                    em("vector", lambda e, k=jj - 1: chain(e, k), inc=c_v)
                    SYM[("CV_QX", sg, jj - 1)] = (c_v, c_v.n)
                    emit_qxT(jj - 1)
            em("vector", lambda e: chain(e, 3), inc=c_v)
            SYM[("CV_QX", sg, 3)] = (c_v, c_v.n)
            emit_qxT(3)

        def emit_front(sg, pre_repack=False):
            front_dma(sg, pre_repack)
            if sg > 0:
                front_stats(sg)
            front_rest(sg)

        def emit_mid(sg):
            for n in range(32):
                fi, ol = n // 2, n % 2
                gbank = psA[:, 0:512] if n % 2 == 0 else psB[:, 0:512]
                ubank = psA[:, 512:1024] if n % 2 == 0 else psB[:, 512:1024]
                wts = [DWS_GU[(sg, fi)]]
                if n == 0:
                    wts.append((c_v, CV_QXEV[(sg, 3)]))
                if n == 12 and sg > 0:
                    wts.append((d_y, 64 * sg))
                if n >= 2:
                    wts += [("CS_RG", (sg, n - 2)), ("CV_A", (sg, n - 2))]
                elif sg > 0:
                    wts.append((c_v, CV_OUT[(sg - 1, 3)]))
                for a in range(8):
                    em("tensor", lambda e, a=a, fi=fi, ol=ol, gbank=gbank: e.matmul(
                        gbank, wg_ring[fi % 3][:, a, 128 * ol:128 * ol + 128],
                        qxT[:, a, :], start=(a == 0), stop=(a == 7)),
                       waits=wts if a == 0 else [])
                for a in range(8):
                    em("tensor", lambda e, a=a, fi=fi, ol=ol, ubank=ubank: e.matmul(
                        ubank, wu_ring[fi % 3][:, a, 128 * ol:128 * ol + 128],
                        qxT[:, a, :], start=(a == 0), stop=(a == 7)),
                       inc=c_t if a == 7 else None)
                CT_GU[(sg, n)] = c_t.n
                SYM[("CT_GU", sg, n)] = (c_t, c_t.n)

                em("scalar", lambda e, n=n, gbank=gbank: e.activation(
                    rg_sb[n % 2][:], gbank, Act.Relu),
                   waits=[(c_t, CT_GU[(sg, n)])], inc=c_s)
                SYM[("CS_RG", sg, n)] = (c_s, c_s.n)
                em("vector", lambda e, n=n, ubank=ubank: e.scalar_tensor_tensor(
                    A_sb[:], ubank, 1.0, rg_sb[n % 2][:], Alu.mult, Alu.mult),
                   waits=[("CS_RG", (sg, n))], inc=c_v)
                SYM[("CV_A", sg, n)] = (c_v, c_v.n)
                hw = [(d_xp, 16 * (n // 2 + 1))] if sg == 0 else []
                em("vector", lambda e, n=n: e.tensor_mul(h_sb[:, n, :], rg_sb[n % 2][:],
                                                         A_sb[:]), waits=hw, inc=c_v)
                if n == 0:
                    em("vector", lambda e: e.tensor_scalar(
                        mxh[:], h_sb[:, 0, :], -1.0, None, Alu.mult))
                else:
                    em("vector", lambda e, n=n: e.scalar_tensor_tensor(
                        mxh[:], h_sb[:, n, :], -1.0, mxh[:], Alu.mult, Alu.max))
                em("vector", lambda e, n=n: e.tensor_tensor(
                    mxh[:], mxh[:], h_sb[:, n, :], Alu.max), inc=c_v)
                if n == 31:
                    SYM[("CV_MIDV", sg)] = (c_v, c_v.n)

        def emit_tok(sg):
            em("vector", lambda e: SEP(e), inc=c_v)
            CV_HLAST = c_v.n
            for b in range(4):
                em("tensor", lambda e, b=b: e.matmul(
                    psT[:, 128 * b:128 * b + 128], mxh[:, 128 * b:128 * b + 128],
                    identf[:], start=True, stop=True),
                   waits=[(c_v, CV_HLAST)] if b == 0 else [],
                   inc=c_t if b == 3 else None)
            CT_MXT = c_t.n

            def tokmath(v):
                for b in range(4):
                    v.reduce_max(st_mxh[:, b:b + 1], psT[:, 128 * b:128 * b + 128], AX)
                SEP(v)
                v.tensor_mul(st_m1[:], st_mxh[:], st_F[:])
                SEP(v)
                v.tensor_scalar(st_m2[:], st_m1[:], EPS, None, Alu.max)
                SEP(v)
                v.reciprocal(st_rc2[:], st_m2[:])
                v.tensor_scalar(st_fd[:], st_m2[:], kconsts[:, 4:5], None, Alu.mult)
                SEP(v)
                v.tensor_mul(st_s0[:], st_rc2[:], st_F[:])
                SEP(v)
                v.tensor_scalar(st_sint[:], st_s0[:], 127.0, None, Alu.mult)
                return SEP(v)
            em("vector", tokmath, waits=[(c_t, CT_MXT)], inc=c_v)
            CV_TOK = c_v.n
            for b in range(4):
                em("tensor", lambda e, b=b: e.matmul(
                    psT[0:1, 128 * b:128 * b + 128], st_sint[:, b:b + 1], identf[:],
                    start=True, stop=True),
                   waits=[(c_v, CV_TOK)] if b == 0 else [], inc=c_t if b == 3 else None)
            CT_ST = c_t.n
            em("vector", lambda e: e.tensor_copy(sseg_row[:, :], psT[0:1, 0:512]),
               waits=[(c_t, CT_ST)])
            em("vector", lambda e: SEP(e), inc=c_v)
            CV_SSEG = c_v.n
            em("tensor", lambda e: e.matmul(
                psT[0:P, 0:512], onesf1[0:1, :], sseg_row[0:1, :],
                start=True, stop=True),
               waits=[(c_v, CV_SSEG)], inc=c_t)
            CT_SBB = c_t.n
            em("vector", lambda e: e.tensor_copy(s_b[:], psT[:, 0:512]),
               waits=[(c_t, CT_SBB)], inc=c_v)
            return c_v.n

        def emit_qh(sg, CV_SBV, v_hook=None):
            CS_QH = {}
            for n in range(32):
                if n == 18 and v_hook is not None:
                    v_hook()
                wts = [(c_v, CV_SBV)]
                if n >= 2:
                    wts.append(("CS_QH", (sg, n - 2)))
                em("vector", lambda e, n=n: e.tensor_mul(qh0[n % 2][:], h_sb[:, n, :],
                                                         s_b[:]),
                   waits=wts, inc=c_v)
                cv_q0 = c_v.n
                em("scalar", lambda e, n=n: e.activation(
                    scrS3[:], qh0[n % 2][:], Act.Identity, bias=cst[:, 1:2],
                    scale=cst[:, 4:5]),
                   waits=[(c_v, cv_q0)])
                qw = []
                if n >= 8:
                    qw.append(("CT_DW", (sg, (n - 8) // 2)))
                elif sg >= 1:
                    qw.append(("CT_DW", (sg - 1, (24 + n) // 2)))
                em("scalar", lambda e, n=n: e.activation(
                    qh_sb[:, n % 8, :], scrS3[:], Act.Identity, bias=cst[:, 6:7],
                    scale=cst[:, 4:5]), waits=qw, inc=c_s)
                CS_QH[n] = c_s.n
                SYM[("CS_QH", sg, n)] = (c_s, c_s.n)
            return CS_QH

        def emit_down(sg):
            bank_of = [psT[:, 0:512], psT[:, 512:1024], psC[:, 0:512], psC[:, 512:1024],
                       psA[:, 0:512], psA[:, 512:1024], psB[:, 0:512], psB[:, 512:1024]]
            for i2 in range(32):
                g = i2 // 2
                wts = [("CS_QH", (sg, i2))]
                if i2 == 0:
                    wts.append(DWS_D[(sg, 0)])
                    if sg > 0:
                        wts.append((c_v, CV_OUT[(sg - 1, 3)]))
                elif i2 % 2 == 0:
                    wts.append(DWS_D[(sg, g)])
                gi = 16 * sg + g
                for b in range(4):
                    for half in range(2):
                        em("tensor", lambda e, i2=i2, b=b, half=half, gi=gi: e.matmul(
                            bank_of[2 * b + half],
                            qh_sb[:, i2 % 8, 128 * b:128 * b + 128],
                            wd_ring[gi % 3][:, i2 % 2, 512 * half:512 * half + 512],
                            start=(i2 == 0), stop=(i2 == 31)),
                           waits=wts if (b == 0 and half == 0) else [],
                           inc=(c_t if (i2 % 2 == 1 and b == 3 and half == 1) else None))
                if i2 % 2 == 1:
                    SYM[("CT_DW", sg, g)] = (c_t, c_t.n)
            CT_DONE[sg] = c_t.n
            SYM[("CT_DONE", sg)] = (c_t, c_t.n)
            return bank_of

        def emit_evac(sg, bank_of):
            for b in range(4):
                def devac(v, sg=sg, b=b):
                    v.scalar_tensor_tensor(out_sb[b % 4][:, 0:512], bank_of[2 * b],
                                           st_fd[:, b:b + 1], res_ring[b % 2][:, 0:512],
                                           Alu.mult, Alu.add)
                    return v.scalar_tensor_tensor(out_sb[b % 4][:, 512:1024],
                                                  bank_of[2 * b + 1],
                                                  st_fd[:, b:b + 1],
                                                  res_ring[b % 2][:, 512:1024],
                                                  Alu.mult, Alu.add)
                wts = [(c_t, CT_DONE[sg]), (d_rs, DRS[(sg, b)])]
                if sg > 0:
                    wts.append((d_y, 16 * (4 * sg + b - 3)))
                em("vector", devac, waits=wts)
                em("vector", lambda e: SEP(e), inc=c_v)
                CV_OUT[(sg, b)] = c_v.n
                if b == 0:
                    SYM[("CV_OUT0", sg)] = (c_v, c_v.n)
            em("vector", lambda e: SEP(e), inc=c_v)
            cv_tail = c_v.n
            for f in FRONT_EVACS:
                f()
            FRONT_EVACS.clear()
            for b in range(4):
                j = 4 * sg + b
                em("scalar", lambda e, j=j, b=b: e.dma_start(
                    out=y_out[j * P:(j + 1) * P, :], in_=out_sb[b % 4][:]),
                   waits=[(c_v, CV_OUT[(sg, b)] + 1 if b < 3 else cv_tail)],
                   inc=d_y, inck=16)

        emit_front(0)
        for sg in range(NSUP):
            emit_mid(sg)
            cv_sbv = emit_tok(sg)
            nxt = sg + 1
            if nxt < NSUP:
                front_dma(nxt, pre_repack=(sg == 0))
                emit_qh(sg, cv_sbv, v_hook=lambda nxt=nxt: front_stats(nxt))
                bank_of = emit_down(sg)
                front_rest(nxt)
            else:
                emit_qh(sg, cv_sbv)
                bank_of = emit_down(sg)
            emit_evac(sg, bank_of)

        em("gpsimd", lambda e: e.engine_nop(), waits=[(d_y, 16 * NT)])

        resolve_all()

        with nc.Block() as block:
            @block.sync
            def _(e):
                W.run("sync", e)

            @block.gpsimd
            def _(e):
                W.run("gpsimd", e)

            @block.tensor
            def _(e):
                W.run("tensor", e)

            @block.vector
            def _(e):
                W.run("vector", e)

            @block.scalar
            def _(e):
                W.run("scalar", e)

    return nc


_NC_CACHE = None


def kernel(L0_attn_x, L0_attn_Wq, L0_attn_Wk, L0_attn_Wv,
           L0_ffn_w_gate, L0_ffn_w_up, L0_ffn_w_down):
    global _NC_CACHE
    x = np.ascontiguousarray(np.asarray(L0_attn_x, dtype=np.float32))
    wq = np.ascontiguousarray(np.asarray(L0_attn_Wq, dtype=np.float32))
    wk = np.ascontiguousarray(np.asarray(L0_attn_Wk, dtype=np.float32))
    wv = np.ascontiguousarray(np.asarray(L0_attn_Wv, dtype=np.float32))
    wg = np.ascontiguousarray(np.asarray(L0_ffn_w_gate, dtype=np.float32))
    wu = np.ascontiguousarray(np.asarray(L0_ffn_w_up, dtype=np.float32))
    wd = np.ascontiguousarray(np.asarray(L0_ffn_w_down, dtype=np.float32))

    in_maps = []
    for i in range(NCORES):
        xi = np.ascontiguousarray(x[:, TPC * i:TPC * (i + 1), :]).reshape(R, E)
        in_maps.append({
            "x": xi, "wq": wq, "wk": wk, "wv": wv,
            "wg_sh": np.ascontiguousarray(wg[512 * i:512 * (i + 1)]),
            "wu_sh": np.ascontiguousarray(wu[512 * i:512 * (i + 1)]),
            "wd_sh": np.ascontiguousarray(wd[128 * i:128 * (i + 1)]),
        })
    if _NC_CACHE is None:
        _NC_CACHE = build_nc()
    res = run_bass_kernel_spmd(_NC_CACHE, in_maps, core_ids=list(range(NCORES)))
    out = np.empty((C, 2048, E), dtype=np.float32)
    for i in range(NCORES):
        out[:, TPC * i:TPC * (i + 1), :] = res.results[i]["y"].reshape(C, TPC, E)
    return out


# version 33
# speedup vs baseline: 1.0068x; 1.0068x over previous
import contextlib
import numpy as np

import concourse.bass as bass
import concourse.mybir as mybir
from concourse.bass_utils import run_bass_kernel_spmd

F32 = mybir.dt.float32
BF16 = mybir.dt.bfloat16
Alu = mybir.AluOpType
Act = mybir.ActivationFunctionType
AX = mybir.AxisListType.X

NCORES = 8
P = 128
C = 8
TPC = 256
R = 2048
NT = 16
E = 1024
I = 4096
NSUP = 4
EPS = 1e-5
MAGIC = 12582912.0
W_COUNT = 4096.0 * 1024.0

OFF_G = 0
OFF_U = 512 * 1024
OFF_D = 1024 * 1024
SHARD = 3 * 512 * 1024


class Sem:
    def __init__(self, handle):
        self.h = handle
        self.n = 0


class Walk:
    def __init__(self):
        self.ops = {k: [] for k in ("sync", "tensor", "vector", "scalar", "gpsimd")}

    def emit(self, eng, fn, waits=(), inc=None, inck=1):
        if inc is not None:
            inc.n += inck
        self.ops[eng].append((fn, tuple(waits), inc, inck))

    def run(self, eng_name, eng):
        for fn, waits, inc, inck in self.ops[eng_name]:
            for s, v in waits:
                eng.wait_ge(s.h, v)
            ins = fn(eng)
            if inc is not None:
                ins.then_inc(inc.h, inck)


def build_nc():
    nc = bass.Bass("TRN2", target_bir_lowering=False, debug=False, detect_race_conditions=False)

    x_in = nc.dram_tensor("x", [R, E], F32, kind="ExternalInput").ap()
    wq_in = nc.dram_tensor("wq", [64, 64], F32, kind="ExternalInput").ap()
    wk_in = nc.dram_tensor("wk", [64, 64], F32, kind="ExternalInput").ap()
    wv_in = nc.dram_tensor("wv", [64, 64], F32, kind="ExternalInput").ap()
    wg_in = nc.dram_tensor("wg_sh", [512, E], F32, kind="ExternalInput").ap()
    wu_in = nc.dram_tensor("wu_sh", [512, E], F32, kind="ExternalInput").ap()
    wd_in = nc.dram_tensor("wd_sh", [128, I], F32, kind="ExternalInput").ap()
    y_out = nc.dram_tensor("y", [R, E], F32, kind="ExternalOutput").ap()

    cc_ar_in = nc.dram_tensor("cc_ar_in", [67, 128], F32)
    cc_ar_out = nc.dram_tensor("cc_ar_out", [67, 128], F32)
    cc_ag_in = nc.dram_tensor("cc_ag_in", [SHARD], BF16)
    cc_ag_out = nc.dram_tensor("cc_ag_out", [NCORES * SHARD], BF16, addr_space="Shared")
    xprime = nc.dram_tensor("xprime", [R, E], F32)
    vsum_dram = nc.dram_tensor("vsum_dram", [C, E], F32)
    wd_repack = nc.dram_tensor("wd_repack", [I, E], BF16)

    with contextlib.ExitStack() as st:
        def sb(name, shape, dtype=F32):
            return st.enter_context(nc.sbuf_tensor(name, shape, dtype))

        def psm(name, shape, dtype=F32):
            return st.enter_context(nc.psum_tensor(name, shape, dtype))

        identf = sb("identf", [P, P], F32)
        identb = sb("identb", [P, P], BF16)
        onesf1 = sb("onesf1", [1, P], F32)
        ones1b = sb("ones1b", [P, 1], BF16)
        cst = sb("cst", [P, 8], F32)
        wqkv = sb("wqkv", [64, 192], F32)
        wqkv_b = sb("wqkv_b", [64, 192], BF16)
        wqkv_t = sb("wqkv_t", [64, 192], BF16)
        BW = sb("BW", [P, 3, P], BF16)
        kconsts = sb("kconsts", [P, 8], F32)
        scal3 = sb("scal3", [3, P], F32)
        srow = sb("srow", [1, 8], F32)
        absn = sb("absn", [P, 12], F32)
        st_sum = sb("st_sum", [P, NT], F32)
        st_ssq = sb("st_ssq", [P, NT], F32)
        st_mean = sb("st_mean", [P, NT], F32)
        st_var = sb("st_var", [P, NT], F32)
        st_rstd = sb("st_rstd", [P, NT], F32)
        st_lnb = sb("st_lnb", [P, NT], F32)
        st_s = sb("st_s", [P, NT * 16], F32)
        st_e = sb("st_e", [P, NT * 16], F32)
        st_p = sb("st_p", [P, NT * 16], F32)
        sm_mx = sb("sm_mx", [P, 32], F32)
        sm_sum = sb("sm_sum", [P, 32], F32)
        sm_r = sb("sm_r", [P, 32], F32)
        se2 = sb("se2", [P, NT * 16], F32)
        xsum_sb = sb("xsum_sb", [C, E], F32)
        xsum_all = sb("xsum_all", [C, E], F32)
        xsumT = sb("xsumT", [P, 64], BF16)
        vsum_sb = sb("vsum_sb", [C, E], F32)
        ind3 = sb("ind3", [P, 8, 8], BF16)
        st2_sum = sb("st2_sum", [P, 4], F32)
        st2_ssq = sb("st2_ssq", [P, 4], F32)
        st2_a = sb("st2_a", [P, 4], F32)
        st2_b = sb("st2_b", [P, 4], F32)
        st2_rstd = sb("st2_rstd", [P, 4], F32)
        st2_lnb = sb("st2_lnb", [P, 4], F32)
        st_mx2 = sb("st_mx2", [P, 4], F32)
        st_c = sb("st_c", [P, 4], F32)
        st_rc = sb("st_rc", [P, 4], F32)
        st_s127 = sb("st_s127", [P, 4], F32)
        st_c3a = sb("st_c3a", [P, 4], F32)
        st_F = sb("st_F", [P, 4], F32)
        st_mxh = sb("st_mxh", [P, 4], F32)
        st_m1 = sb("st_m1", [P, 4], F32)
        st_m2 = sb("st_m2", [P, 4], F32)
        st_rc2 = sb("st_rc2", [P, 4], F32)
        st_s0 = sb("st_s0", [P, 4], F32)
        st_sint = sb("st_sint", [P, 4], F32)
        st_fd = sb("st_fd", [P, 4], F32)
        sseg_row = sb("sseg_row", [1, 512], F32)
        mxh = sb("mxh", [P, 512], F32)
        s_b = sb("s_b", [P, 512], F32)
        sepb = sb("sepb", [P, 512], F32)

        AW = 45232
        arena = sb("arena", [P, AW], F32)

        class Alloc:
            def __init__(self):
                self.off = 0

            def f32(self, words):
                o = self.off
                assert o + words <= AW, f"arena overflow {o}+{words}"
                self.off = o + words
                return arena[:, o:o + words]

            def bf16(self, elems):
                assert elems % 2 == 0
                return self.f32(elems // 2).bitcast(BF16)

        aa = Alloc()
        xr = [aa.f32(E) for _ in range(NT)]
        wshg_f = aa.f32(4 * E)
        wshg = wshg_f.rearrange("p (t i) -> p t i", i=E)
        wshu_f = aa.f32(4 * E)
        wshu = wshu_f.rearrange("p (t i) -> p t i", i=E)
        wshd = aa.f32(I)
        qshg = aa.bf16(4 * E).rearrange("p (t i) -> p t i", i=E)
        qshu = aa.bf16(4 * E).rearrange("p (t i) -> p t i", i=E)
        qshd = aa.bf16(I)
        qwgT = aa.bf16(8 * 512).rearrange("p (m o) -> p m o", o=512)
        qwuT = aa.bf16(8 * 512).rearrange("p (m o) -> p m o", o=512)
        qwdT = aa.bf16(32 * 128).rearrange("p (m o) -> p m o", o=128)
        scrS = aa.f32(E)
        x1buf = [scrS[:, 0:512].bitcast(BF16), scrS[:, 512:1024].bitcast(BF16)]
        x1T = [aa.bf16(E) for _ in range(2)]
        q_sb = aa.f32(E)
        scrV = aa.f32(E)
        qk_scr = scrV
        assert aa.off <= AW, aa.off

        fa = Alloc()
        h_raw = fa.f32(32 * 512)
        h_sb = h_raw.rearrange("p (n o) -> p n o", o=512)
        xpraw = h_raw[:, 0:4096]
        xp_buf = xpraw.rearrange("p (t i) -> p t i", i=E)
        qh_raw = fa.f32(8 * 256)
        qh_sb = qh_raw.bitcast(BF16).rearrange("p (n o) -> p n o", o=512)
        xst_raw = fa.f32(4096)
        xstage = [xst_raw[:, 0:2048].bitcast(BF16).rearrange("p (m o) -> p m o", o=128),
                  xst_raw[:, 2048:4096].bitcast(BF16).rearrange("p (m o) -> p m o", o=128)]
        qxT = fa.bf16(8 * 512).rearrange("p (m o) -> p m o", o=512)
        wg_ring = [fa.bf16(8 * 256).rearrange("p (m o) -> p m o", o=256) for _ in range(3)]
        wu_ring = [fa.bf16(8 * 256).rearrange("p (m o) -> p m o", o=256) for _ in range(3)]
        wd_ring = [fa.bf16(2 * 1024).rearrange("p (m o) -> p m o", o=1024) for _ in range(3)]
        scr_x2 = fa.f32(E)
        x2b = fa.f32(E)
        scrV2 = fa.f32(E)
        rgraw = fa.f32(1024)
        rg_sb = [rgraw[:, 0:512], rgraw[:, 512:1024]]
        A_sb = fa.f32(512)
        qh0raw = fa.f32(1024)
        x2c = rgraw
        x2d = qh0raw
        qh0 = [qh0raw[:, 0:512], qh0raw[:, 512:1024]]
        scrS3 = fa.f32(512)
        out_sb = [fa.f32(E) for _ in range(2)]
        out_sb.append(h_raw[:, 6144:7168])
        out_sb.append(h_raw[:, 7168:8192])
        res_ring = [fa.f32(E) for _ in range(2)]
        x2s = [scr_x2, x2b, x2c, x2d]
        qxs = [s[:, 0:512].bitcast(BF16) for s in x2s]
        assert fa.off <= AW, fa.off

        def qx_slot(jj):
            return qxs[jj % 4]

        psT = psm("psT", [P, 1024], F32)
        psA = psm("psA", [P, 1024], F32)
        psB = psm("psB", [P, 1024], F32)
        psC = psm("psC", [P, 1024], F32)

        names = ("d_x", "d_w", "d_cc", "d_cb", "d_xp", "d_xl", "d_ws", "d_rs", "d_y", "d_v",
                 "d_b0", "d_b1", "d_g0", "d_g1", "d_d0", "d_d1", "d_d2",
                 "c_t", "c_v", "c_s", "c_g", "s_cc")
        S = {k: Sem(st.enter_context(nc.semaphore(k))) for k in names}
        d_x, d_w, d_cc, d_cb, d_xp, d_xl, d_ws, d_rs, d_y, d_v = (S[k] for k in
            ("d_x", "d_w", "d_cc", "d_cb", "d_xp", "d_xl", "d_ws", "d_rs", "d_y", "d_v"))
        d_b = [S["d_b0"], S["d_b1"]]
        d_g = [S["d_g0"], S["d_g1"]]
        d_d = [S["d_d0"], S["d_d1"], S["d_d2"]]
        c_t, c_v, c_s, c_g, s_cc = (S[k] for k in ("c_t", "c_v", "c_s", "c_g", "s_cc"))

        W = Walk()
        em = W.emit

        def SEP(v):
            return v.tensor_copy(sepb[:, 0:512], mxh[:, 0:512])

        SYM = {}

        def resolve_all():
            for eng in W.ops:
                out = []
                for fn, waits, inc, inck in W.ops[eng]:
                    nw = []
                    for s, v in waits:
                        if isinstance(s, str):
                            sem, val = SYM[(s,) + (v if isinstance(v, tuple) else (v,))]
                            nw.append((sem, val))
                        else:
                            nw.append((s, v))
                    out.append((fn, tuple(nw), inc, inck))
                W.ops[eng] = out

        def mark(tag, *key):
            pass

        em("sync", lambda e: e.dma_start(out=wqkv[:, 0:64], in_=wq_in), inc=d_w, inck=16)
        em("sync", lambda e: e.dma_start(out=wqkv[:, 64:128], in_=wk_in), inc=d_w, inck=16)
        em("sync", lambda e: e.dma_start(out=wqkv[:, 128:192], in_=wv_in), inc=d_w, inck=16)
        for j in range(NT):
            em("sync", lambda e, j=j: e.dma_start(
                out=xr[j], in_=x_in[j * P:(j + 1) * P, :]), inc=d_x, inck=16)
        em("sync", lambda e: e.dma_start(
            out=wshg, in_=wg_in.rearrange("(t p) i -> p t i", p=P)), inc=d_w, inck=16)
        em("sync", lambda e: e.dma_start(
            out=wshu, in_=wu_in.rearrange("(t p) i -> p t i", p=P)), inc=d_w, inck=16)
        em("sync", lambda e: e.dma_start(out=wshd, in_=wd_in), inc=d_w, inck=16)
        DW_IN = d_w.n

        def vconsts(v):
            v.memset(sepb[:], 0.0)
            v.memset(mxh[:], 0.0)
            v.memset(identf[:], 0.0)
            v.memset(onesf1[:], 1.0)
            v.memset(ones1b[:], 1.0)
            v.memset(cst[:, 0:1], EPS)
            v.memset(cst[:, 1:2], MAGIC)
            v.memset(cst[:, 2:3], -0.5)
            v.memset(cst[:, 3:4], 1.0 / 32.0)
            v.memset(cst[:, 4:5], 1.0)
            v.memset(cst[:, 5:6], 0.0)
            v.memset(cst[:, 6:7], -MAGIC)
            v.memset(BW[:], 0.0)
            v.memset(ind3[:], 0.0)
            v.memset(scal3[:], 0.0)
            v.memset(srow[:], 0.0)
            SEP(v)
            for _c in range(8):
                v.memset(ind3[:, _c, _c:_c + 1], 1.0)
            return SEP(v)
        em("vector", vconsts, inc=c_v)
        CV_CONST = c_v.n

        def gconsts(g):
            g.wait_ge(c_v.h, CV_CONST)
            return g.affine_select(out=identf[:], in_=identf[:],
                                   compare_op=Alu.not_equal, fill=1.0, base=0,
                                   pattern=[[-1, P]], channel_multiplier=1)
        em("gpsimd", gconsts, inc=c_g)
        CG = c_g.n

        em("vector", lambda e: e.tensor_copy(identb[:], identf[:]),
           waits=[(c_g, CG)])
        em("vector", lambda e: SEP(e), inc=c_v)
        CV_IDB = c_v.n
        em("vector", lambda e: e.tensor_copy(wqkv_b[:], wqkv[:]),
           waits=[(d_w, DW_IN)], inc=c_v)
        CV_WQKVB = c_v.n
        for w in range(3):
            em("tensor", lambda e, w=w: e.matmul(
                psT[0:64, 64 * w:64 * w + 64], wqkv_b[:, 64 * w:64 * w + 64],
                identb[0:64, 0:64], start=True, stop=True),
               waits=[(c_v, CV_WQKVB)] if w == 0 else [],
               inc=c_t if w == 2 else None)
        CT_WQT = c_t.n
        em("vector", lambda e: e.tensor_copy(wqkv_t[:, :], psT[0:64, 0:192]),
           waits=[(c_t, CT_WQT)])
        em("vector", lambda e: SEP(e), inc=c_v)
        CV_WQT = c_v.n
        for w in range(3):
            em("sync", lambda e, w=w: e.dma_start(
                out=BW[0:64, w, 0:64], in_=wqkv_t[:, 64 * w:64 * w + 64]),
               waits=[(c_v, CV_WQT)] if w == 0 else [], inc=d_w, inck=16)
            em("sync", lambda e, w=w: e.dma_start(
                out=BW[64:128, w, 64:128], in_=wqkv_t[:, 64 * w:64 * w + 64]),
               inc=d_w, inck=16)
        DW_BW = d_w.n

        CV_LNB1G = [0] * 4
        for g4 in range(4):
            sl = slice(4 * g4, 4 * g4 + 4)
            for j in range(4 * g4, 4 * g4 + 4):
                em("vector", lambda e, j=j: e.scalar_tensor_tensor(
                    scrV[:], xr[j], 1.0, xr[j], Alu.mult, Alu.mult,
                    accum_out=st_ssq[:, j:j + 1]),
                   waits=[(d_x, 16 * (4 * g4 + 4))] if j % 4 == 0 else [])
                em("vector", lambda e, j=j: e.tensor_scalar(
                    scrV[:], xr[j], 1.0, 0.0, Alu.mult, Alu.add,
                    accum_out=st_sum[:, j:j + 1]))

            def ln1math(v, sl=sl):
                SEP(v)
                v.tensor_scalar(st_mean[:, sl], st_sum[:, sl], 1.0 / E, None, Alu.mult)
                v.tensor_mul(st_lnb[:, sl], st_sum[:, sl], st_sum[:, sl])
                v.tensor_scalar(st_var[:, sl], st_ssq[:, sl], 1.0 / E, None, Alu.mult)
                SEP(v)
                v.tensor_scalar(st_lnb[:, sl], st_lnb[:, sl], 1.0 / (E * E), None,
                                Alu.mult)
                SEP(v)
                v.tensor_sub(st_var[:, sl], st_var[:, sl], st_lnb[:, sl])
                return SEP(v)
            em("vector", ln1math, inc=c_v)
            CV_LN1V = c_v.n

            def ln1rstd(s, sl=sl):
                s.activation(st_lnb[:, sl], st_var[:, sl], Act.Ln, bias=cst[:, 0:1],
                             scale=cst[:, 4:5])
                return s.activation(st_rstd[:, sl], st_lnb[:, sl], Act.Exp,
                                    bias=cst[:, 5:6], scale=cst[:, 2:3])
            em("scalar", ln1rstd, waits=[(c_v, CV_LN1V)], inc=c_s)
            CS_RSTD1 = c_s.n
            em("vector", lambda e, sl=sl: e.scalar_tensor_tensor(
                st_lnb[:, sl], st_mean[:, sl], -1.0, st_rstd[:, sl],
                Alu.mult, Alu.mult),
               waits=[(c_s, CS_RSTD1)])
            em("vector", lambda e: SEP(e), inc=c_v)
            CV_LNB1G[g4] = c_v.n

        def wabs(v):
            v.reduce_sum(absn[:, 0:1], wshg_f, AX, apply_absolute_value=True)
            v.reduce_sum(absn[:, 1:2], wshu_f, AX, apply_absolute_value=True)
            v.reduce_sum(absn[:, 2:3], wshd[:, :], AX, apply_absolute_value=True)
            return SEP(v)
        em("vector", wabs, waits=[(d_w, DW_IN)], inc=c_v)
        CV_WABS = c_v.n

        CS_X1 = [0] * NT
        CT_TT = [0] * NT
        CT_QK = [0] * NT
        CV_EVX = [0] * NT
        CV_QK = [0] * NT

        for j in range(NT):
            waits = [(c_v, CV_LNB1G[j // 4])]
            if j >= 2:
                waits.append(("CT_QK", j - 2))
            em("scalar", lambda e, j=j: e.activation(
                x1buf[j % 2][:], xr[j], Act.Identity,
                scale=st_rstd[:, j:j + 1], bias=st_lnb[:, j:j + 1]),
               waits=waits, inc=c_s)
            CS_X1[j] = c_s.n
            SYM[("CS_X1", j)] = (c_s, c_s.n)

        for j in range(NT):
            waits = [("CS_X1", j), (c_v, CV_IDB)]
            if j >= 1:
                waits.append(("CV_EVX", j - 1))
            for m in range(8):
                em("tensor", lambda e, j=j, m=m: e.matmul(
                    psT[:, 128 * m:128 * m + 128], x1buf[j % 2][:, 128 * m:128 * m + 128],
                    identb[:], start=True, stop=True),
                   waits=waits if m == 0 else [], inc=c_t if m == 7 else None)
            CT_TT[j] = c_t.n
            SYM[("CT_TT", j)] = (c_t, c_t.n)

            w_q = [(d_w, DW_BW), ("CV_EVX", j)]
            if j >= 1:
                w_q.append(("CV_QK", j - 1))
            for m in range(8):
                em("tensor", lambda e, j=j, m=m: e.matmul(
                    psA[:, 128 * m:128 * m + 128], x1T[j % 2][:, 128 * m:128 * m + 128],
                    BW[:, 0, :], start=True, stop=True),
                   waits=w_q if m == 0 else [])
            for m in range(8):
                em("tensor", lambda e, j=j, m=m: e.matmul(
                    psB[:, 128 * m:128 * m + 128], x1T[j % 2][:, 128 * m:128 * m + 128],
                    BW[:, 1, :], start=True, stop=True))
            c = j // 2
            for half in range(2):
                em("tensor", lambda e, j=j, c=c, half=half: e.matmul(
                    psC[0:8, 512 * half:512 * half + 512], ind3[:, c, :],
                    x1buf[j % 2][:, 512 * half:512 * half + 512],
                    start=(j == 0), stop=(j == NT - 1)),
                   inc=c_t if half == 1 else None)
            CT_QK[j] = c_t.n
            SYM[("CT_QK", j)] = (c_t, c_t.n)

            em("vector", lambda e, j=j: e.tensor_copy(x1T[j % 2][:], psT[:, :]),
               waits=[(c_t, CT_TT[j])], inc=c_v)
            CV_EVX[j] = c_v.n
            SYM[("CV_EVX", j)] = (c_v, c_v.n)
            em("vector", lambda e, j=j: e.tensor_copy(q_sb[:], psA[:, :]),
               waits=[(c_t, CT_QK[j])], inc=c_v)
            em("vector", lambda e, j=j: e.tensor_mul(qk_scr[:], q_sb[:], psB[:, :]))
            em("vector", lambda e, j=j: e.reduce_sum(
                st_s[:, 16 * j:16 * j + 16].rearrange("p (a b) -> p a b", b=1),
                qk_scr[:].rearrange("p (a b) -> p a b", b=64), AX), inc=c_v)
            CV_QK[j] = c_v.n
            SYM[("CV_QK", j)] = (c_v, c_v.n)

        em("tensor", lambda e: e.matmul(psT[0:3, 0:128], absn[:, 0:3], identf[:],
                                        start=True, stop=True),
           waits=[(c_v, CV_WABS), (c_v, CV_EVX[NT - 1])], inc=c_t)
        CT_WPT = c_t.n
        em("vector", lambda e: e.reduce_sum(scal3[0:3, 0:1], psT[0:3, 0:128], AX),
           waits=[(c_t, CT_WPT)], inc=c_v)
        em("vector", lambda e: e.tensor_copy(xsum_sb[:, :], psC[0:8, :]),
           waits=[(c_t, CT_QK[NT - 1])])
        em("vector", lambda e: SEP(e), inc=c_v)
        CV_CCRDY = c_v.n
        em("sync", lambda e: e.dma_start(
            out=cc_ar_in.ap()[0:64, :].rearrange("(c k) f -> c (k f)", k=8),
            in_=xsum_sb[:, :]), waits=[(c_v, CV_CCRDY)], inc=d_cc, inck=16)
        em("sync", lambda e: e.dma_start(out=cc_ar_in.ap()[64:67, :], in_=scal3[:, :]),
           inc=d_cc, inck=16)
        DCC_IN = d_cc.n
        em("gpsimd", lambda e: e.collective_compute(
            "AllReduce", Alu.add, replica_groups=[list(range(NCORES))],
            ins=[cc_ar_in.ap()], outs=[cc_ar_out.ap()]),
           waits=[(d_cc, DCC_IN)], inc=s_cc)
        SCC_AR = s_cc.n
        em("gpsimd", lambda e: e.dma_start(
            out=xsum_all[:, :],
            in_=cc_ar_out.ap()[0:64, :].rearrange("(c k) f -> c (k f)", k=8)),
           waits=[(s_cc, SCC_AR)], inc=d_cb, inck=16)
        em("gpsimd", lambda e: e.dma_start(out=scal3[0:3, 0:128],
                                           in_=cc_ar_out.ap()[64:67, 0:128]),
           inc=d_cb, inck=16)
        DCC_BACK = d_cb.n

        em("tensor", lambda e: e.matmul(psT[0:1, 0:3], scal3[0:3, 0:1], identf[0:3, 0:3],
                                        start=True, stop=True),
           waits=[(d_cb, DCC_BACK)], inc=c_t)
        CT_S3 = c_t.n

        def smath(v):
            v.tensor_copy(srow[:, 0:3], psT[0:1, 0:3])
            SEP(v)
            v.tensor_scalar(srow[:, 0:3], srow[:, 0:3], 1.0 / W_COUNT, EPS, Alu.mult, Alu.max)
            SEP(v)
            v.tensor_mul(srow[:, 3:4], srow[:, 0:1], srow[:, 0:1])
            v.tensor_scalar(srow[:, 4:5], srow[:, 2:3], 1.0 / 127.0, None, Alu.mult)
            v.reciprocal(srow[:, 5:8], srow[:, 0:3])
            SEP(v)
            v.tensor_mul(srow[:, 3:4], srow[:, 3:4], srow[:, 1:2])
            SEP(v)
            v.tensor_scalar(srow[:, 3:4], srow[:, 3:4], 1.0 / 2048383.0, None, Alu.mult)
            v.tensor_copy(srow[:, 0:1], srow[:, 5:6])
            v.tensor_copy(srow[:, 1:2], srow[:, 6:7])
            v.tensor_copy(srow[:, 2:3], srow[:, 7:8])
            return SEP(v)
        em("vector", smath, waits=[(c_t, CT_S3)], inc=c_v)
        CV_SMATH = c_v.n
        em("tensor", lambda e: e.matmul(psT[0:P, 0:8], onesf1[0:1, :], srow[0:1, 0:8],
                                        start=True, stop=True),
           waits=[(c_v, CV_SMATH)], inc=c_t)
        CT_KC = c_t.n
        em("vector", lambda e: e.tensor_copy(kconsts[:, :], psT[0:P, 0:8]),
           waits=[(c_t, CT_KC)])
        em("vector", lambda e: SEP(e), inc=c_v)
        CV_KC = c_v.n

        qv_done = {}

        def quant(v, wsrc, qdst, col):
            v.tensor_scalar(scrV[:, 0:wsrc.shape[-1]], wsrc,
                            kconsts[:, col:col + 1], MAGIC, Alu.mult, Alu.add)
            v.tensor_scalar(scrV[:, 0:wsrc.shape[-1]], scrV[:, 0:wsrc.shape[-1]],
                            MAGIC, -1.0, Alu.subtract, Alu.max)
            return v.tensor_scalar(qdst, scrV[:, 0:wsrc.shape[-1]], 1.0, None, Alu.min)
        for t in range(4):
            em("vector", lambda e, t=t: quant(e, wshg[:, t, :], qshg[:, t, :], 0),
               waits=[(c_v, CV_KC)] if t == 0 else [], inc=c_v)
            qv_done[("g", t)] = c_v.n
        for t in range(4):
            em("vector", lambda e, t=t: quant(e, wshu[:, t, :], qshu[:, t, :], 1), inc=c_v)
            qv_done[("u", t)] = c_v.n
        for t in range(4):
            em("vector", lambda e, t=t: quant(
                e, wshd[:, 1024 * t:1024 * (t + 1)], qshd[:, 1024 * t:1024 * (t + 1)], 2),
               inc=c_v)
            qv_done[("d", t)] = c_v.n

        CV_WEV = []

        def wtr_batch(which, blocks, evac):
            for i, (lhs, slot) in enumerate(blocks):
                wts = []
                if i == 0:
                    wts = [(c_v, qv_done[which]),
                           (c_v, CV_WEV[-1] if CV_WEV else CV_KC)]
                em("tensor", lambda e, lhs=lhs, slot=slot: e.matmul(
                    psT[:, 128 * slot:128 * slot + 128], lhs, identb[:],
                    start=True, stop=True),
                   waits=wts, inc=c_t if i == len(blocks) - 1 else None)
            ct = c_t.n
            em("vector", lambda e, evac=evac: e.tensor_copy(
                evac, psT[:, :].rearrange("p (m o) -> p m o", o=128)),
               waits=[(c_t, ct)], inc=c_v)
            CV_WEV.append(c_v.n)

        for t in range(4):
            wtr_batch(("g", t), [(qshg[:, t, 128 * m:128 * m + 128], m) for m in range(8)],
                      qwgT[:, :, 128 * t:128 * t + 128])
        for t in range(4):
            wtr_batch(("u", t), [(qshu[:, t, 128 * m:128 * m + 128], m) for m in range(8)],
                      qwuT[:, :, 128 * t:128 * t + 128])
        for t in range(4):
            wtr_batch(("d", t),
                      [(qshd[:, 128 * (8 * t + mm):128 * (8 * t + mm) + 128], mm)
                       for mm in range(8)],
                      qwdT[:, 8 * t:8 * t + 8, :])
        em("vector", lambda e: SEP(e), inc=c_v)
        CV_WEV.append(c_v.n)
        CV_WTR_LAST = CV_WEV[-1]

        bmask = wshu_f[0:16, 0:E]
        indf = wshg_f[0:8, 0:128]

        def bm_init(v):
            v.memset(bmask[:, :], 1.0)
            return v.memset(indf[:, :], 1.0)
        em("vector", bm_init, waits=[(c_v, CV_WTR_LAST)], inc=c_v)
        CV_BM1 = c_v.n

        def bmask_sel(g):
            g.affine_select(out=bmask[:, :], in_=bmask[:, :],
                            compare_op=Alu.is_ge, fill=0.0, base=0,
                            pattern=[[1, E]], channel_multiplier=-64)
            g.affine_select(out=bmask[:, :], in_=bmask[:, :],
                            compare_op=Alu.is_ge, fill=0.0, base=63,
                            pattern=[[-1, E]], channel_multiplier=64)
            g.affine_select(out=indf[:, :], in_=indf[:, :],
                            compare_op=Alu.is_ge, fill=0.0, base=0,
                            pattern=[[1, 128]], channel_multiplier=-16)
            return g.affine_select(out=indf[:, :], in_=indf[:, :],
                                   compare_op=Alu.is_ge, fill=0.0, base=15,
                                   pattern=[[-1, 128]], channel_multiplier=16)
        em("gpsimd", bmask_sel, waits=[(c_v, CV_BM1)], inc=c_g)
        CG_BM = c_g.n

        ag = cc_ag_in.ap()
        em("sync", lambda e: e.dma_start(
            out=ag[OFF_G:OFF_G + 512 * 1024].rearrange("(m p o) -> p m o", p=P, o=512),
            in_=qwgT), waits=[(c_v, CV_WTR_LAST)], inc=d_cc, inck=16)
        em("sync", lambda e: e.dma_start(
            out=ag[OFF_U:OFF_U + 512 * 1024].rearrange("(m p o) -> p m o", p=P, o=512),
            in_=qwuT), inc=d_cc, inck=16)
        em("sync", lambda e: e.dma_start(
            out=ag[OFF_D:OFF_D + 512 * 1024].rearrange("(m p o) -> p m o", p=P, o=128),
            in_=qwdT), inc=d_cc, inck=16)
        DCC_AG = d_cc.n
        em("gpsimd", lambda e: e.collective_compute(
            "AllGather", Alu.bypass, replica_groups=[list(range(NCORES))],
            ins=[cc_ag_in.ap()], outs=[cc_ag_out.ap()]),
           waits=[(d_cc, DCC_AG)], inc=s_cc)
        SCC_AG = s_cc.n

        def softmax1(v):
            SEP(v)
            for b in range(2):
                for q4 in range(4):
                    j0, j1 = 4 * q4 + b, 4 * q4 + 2 + b
                    v.tensor_tensor(se2[:, 16 * (2 * q4 + b):16 * (2 * q4 + b) + 16],
                                    st_s[:, 16 * j0:16 * j0 + 16],
                                    st_s[:, 16 * j1:16 * j1 + 16], Alu.max)
            for b in range(2):
                for q2 in range(2):
                    v.tensor_tensor(
                        se2[:, 128 + 16 * (2 * q2 + b):128 + 16 * (2 * q2 + b) + 16],
                        se2[:, 16 * (4 * q2 + b):16 * (4 * q2 + b) + 16],
                        se2[:, 16 * (4 * q2 + 2 + b):16 * (4 * q2 + 2 + b) + 16],
                        Alu.max)
            SEP(v)
            for b in range(2):
                v.tensor_tensor(sm_mx[:, 16 * b:16 * b + 16],
                                se2[:, 128 + 16 * b:128 + 16 * b + 16],
                                se2[:, 128 + 16 * (2 + b):128 + 16 * (2 + b) + 16],
                                Alu.max)
            SEP(v)
            for j in range(NT):
                b = j % 2
                v.tensor_sub(st_e[:, 16 * j:16 * j + 16],
                             st_s[:, 16 * j:16 * j + 16], sm_mx[:, 16 * b:16 * b + 16])
            return SEP(v)
        em("vector", softmax1, waits=[(c_v, CV_QK[NT - 1])], inc=c_v)
        CV_SM1 = c_v.n
        em("scalar", lambda e: e.activation(se2[:], st_e[:], Act.Exp,
                                            bias=cst[:, 5:6], scale=cst[:, 3:4]),
           waits=[(c_v, CV_SM1)], inc=c_s)
        CS_EXP = c_s.n

        def softmax2(v):
            for b in range(2):
                for q4 in range(4):
                    j0, j1 = 4 * q4 + b, 4 * q4 + 2 + b
                    v.tensor_add(st_e[:, 16 * (2 * q4 + b):16 * (2 * q4 + b) + 16],
                                 se2[:, 16 * j0:16 * j0 + 16],
                                 se2[:, 16 * j1:16 * j1 + 16])
            for b in range(2):
                for q2 in range(2):
                    v.tensor_add(
                        st_e[:, 128 + 16 * (2 * q2 + b):128 + 16 * (2 * q2 + b) + 16],
                        st_e[:, 16 * (4 * q2 + b):16 * (4 * q2 + b) + 16],
                        st_e[:, 16 * (4 * q2 + 2 + b):16 * (4 * q2 + 2 + b) + 16])
            SEP(v)
            for b in range(2):
                v.tensor_add(sm_sum[:, 16 * b:16 * b + 16],
                             st_e[:, 128 + 16 * b:128 + 16 * b + 16],
                             st_e[:, 128 + 16 * (2 + b):128 + 16 * (2 + b) + 16])
            SEP(v)
            v.reciprocal(sm_r[:, 0:32], sm_sum[:, 0:32])
            SEP(v)
            for j in range(NT):
                b = j % 2
                v.tensor_mul(st_p[:, 16 * j:16 * j + 16], se2[:, 16 * j:16 * j + 16],
                             sm_r[:, 16 * b:16 * b + 16])
            return SEP(v)
        em("vector", softmax2, waits=[(c_s, CS_EXP)], inc=c_v)
        CV_P = c_v.n

        for m in range(8):
            em("tensor", lambda e, m=m: e.matmul(
                psT[0:P, 8 * m:8 * m + 8], xsum_all[:, 128 * m:128 * m + 128],
                identf[0:8, 0:8], start=True, stop=True),
               waits=[(d_cb, DCC_BACK), (c_v, CV_WTR_LAST)] if m == 0 else [],
               inc=c_t if m == 7 else None)
        CT_XST = c_t.n
        em("vector", lambda e: e.tensor_copy(xsumT[:, :], psT[:, 0:64]),
           waits=[(c_t, CT_XST)])
        em("vector", lambda e: SEP(e), inc=c_v)
        CV_XST = c_v.n
        for m in range(8):
            em("tensor", lambda e, m=m: e.matmul(
                psA[0:8, 128 * m:128 * m + 128], xsumT[:, 8 * m:8 * m + 8],
                BW[:, 2, :], start=True, stop=True),
               waits=[(c_v, CV_XST)] if m == 0 else [], inc=c_t if m == 7 else None)
        CT_VS = c_t.n
        em("vector", lambda e: e.tensor_copy(vsum_sb[:, :], psA[0:8, :]),
           waits=[(c_t, CT_VS)])
        em("vector", lambda e: SEP(e), inc=c_v)
        CV_VS = c_v.n
        CV_ADD = [0] * NT
        DXP = [0] * NT

        def emit_ln2_stats(jj_waits):
            for jj in range(4):
                wts = list(jj_waits[jj])
                em("vector", lambda e, jj=jj: e.scalar_tensor_tensor(
                    scrV2[:], xp_buf[:, jj, :], 1.0, xp_buf[:, jj, :],
                    Alu.mult, Alu.mult, accum_out=st2_ssq[:, jj:jj + 1]),
                   waits=wts)
                em("vector", lambda e, jj=jj: e.tensor_scalar(
                    scrV2[:], xp_buf[:, jj, :], 1.0, 0.0, Alu.mult, Alu.add,
                    accum_out=st2_sum[:, jj:jj + 1]))

            def ln2math(v):
                SEP(v)
                v.tensor_scalar(st2_a[:], st2_sum[:], 1.0 / E, None, Alu.mult)
                v.tensor_mul(st2_b[:], st2_sum[:], st2_sum[:])
                v.tensor_scalar(st2_rstd[:], st2_ssq[:], 1.0 / E, None, Alu.mult)
                SEP(v)
                v.tensor_scalar(st2_b[:], st2_b[:], 1.0 / (E * E), None, Alu.mult)
                SEP(v)
                v.tensor_sub(st2_rstd[:], st2_rstd[:], st2_b[:])
                return SEP(v)
            em("vector", ln2math, inc=c_v)
            return c_v.n

        vmat = scrV[0:16, :]
        pt_sl = [se2[0:16, 0:128], se2[0:16, 128:256]]
        CV_PTEV = {}
        CT_ATTN = {}
        CV_ADDR = {}
        for j in range(NT):
            c = j // 2
            if j % 2 == 0:
                wts = [(c_v, CV_VS)] if c == 0 else [(c_v, CV_VM)]
                if c == 0:
                    wts.append((c_g, CG_BM))
                for half in range(2):
                    em("tensor", lambda e, c=c, half=half: e.matmul(
                        psA[0:16, 512 * half:512 * half + 512],
                        indf[:, 16 * c:16 * c + 16],
                        vsum_sb[0:8, 512 * half:512 * half + 512],
                        start=True, stop=True),
                       waits=wts if half == 0 else [],
                       inc=c_t if half == 1 else None)
                ct_rep = c_t.n
                em("vector", lambda e: e.tensor_mul(
                    vmat, psA[0:16, :], bmask[:, :]),
                   waits=[(c_t, ct_rep)], inc=c_v)
                CV_VM = c_v.n
            wts = [(c_v, CV_PTEV[j - 4])] if j >= 4 else [(c_v, CV_XST)]
            if j == 0:
                wts.append((c_v, CV_P))
            em("tensor", lambda e, j=j: e.matmul(
                psT[0:16, 128 * (j % 4):128 * (j % 4) + 128],
                st_p[:, 16 * j:16 * j + 16], identf[:], start=True, stop=True),
               waits=wts, inc=c_t)
            ct_pt = c_t.n
            em("vector", lambda e, j=j: e.tensor_copy(
                pt_sl[j % 2], psT[0:16, 128 * (j % 4):128 * (j % 4) + 128]),
               waits=[(c_t, ct_pt)], inc=c_v)
            CV_PTEV[j] = c_v.n
            psX = psB if j % 2 == 0 else psC
            for half in range(2):
                wts = [(c_v, CV_PTEV[j]), (c_v, CV_VM)] if half == 0 else []
                if half == 0 and j >= 2:
                    wts.append((c_v, CV_ADDR[j - 2]))
                em("tensor", lambda e, j=j, half=half, psX=psX: e.matmul(
                    psX[:, 512 * half:512 * half + 512],
                    pt_sl[j % 2], vmat[:, 512 * half:512 * half + 512],
                    start=True, stop=True),
                   waits=wts, inc=c_t if half == 1 else None)
            CT_ATTN[j] = c_t.n
            em("vector", lambda e, j=j, psX=psX: e.tensor_add(
                xr[j][:, :], psX[:, :], xr[j][:, :]),
               waits=[(c_t, CT_ATTN[j])], inc=c_v)
            CV_ADDR[j] = c_v.n
            em("vector", lambda e: SEP(e), inc=c_v)
            CV_ADD[j] = c_v.n
            SYM[("CV_ADD", j)] = (c_v, c_v.n)
            em("sync", lambda e, j=j: e.dma_start(
                out=xprime.ap()[j * P:(j + 1) * P, :], in_=xr[j]),
               waits=[(c_v, CV_ADD[j])], inc=d_xp, inck=16)
            DXP[j] = d_xp.n
            if j == 3:
                CV_LN2_0 = emit_ln2_stats([[(d_cc, DCC_AG)], [], [], []])

        ago = cc_ag_out.ap().rearrange("(r q) -> r q", q=SHARD)

        def fetch_gu(e, ring, r, hb, off):
            src2 = ago[:, off:off + 512 * 1024].rearrange(
                "r (m p o) -> r m p o", p=P, o=512)[r, :, :, 256 * hb:256 * hb + 256]
            return e.dma_start(out=ring, in_=src2.transpose([1, 0, 2]))

        wdr = wd_repack.ap().rearrange("(m p) o -> m p o", p=P)

        def fetch_wd(e, ring, g):
            return e.dma_start(
                out=ring, in_=wdr[2 * g:2 * g + 2].transpose([1, 0, 2]))

        CT_GU = {}
        CV_OUT = {}
        CV_QXEV = {}
        FRONT_EVACS = []
        CT_DONE = [0] * NSUP
        DWS_GU = {}
        DWS_D = {}
        DRS = {}

        agd = ago[:, OFF_D:OFF_D + 512 * 1024].rearrange(
            "r (m p o) -> r m p o", p=P, o=128)

        def emit_repack():
            DWS_LD = {}
            DWS_ST = {}
            for r in range(8):
                wts = []
                if r == 0:
                    wts = [(s_cc, SCC_AG)]
                elif r >= 2:
                    wts = [(d_ws, DWS_ST[r - 2])]
                em("sync", lambda e, r=r: e.dma_start(
                    out=xstage[r % 2], in_=agd[r].transpose([1, 0, 2])),
                   waits=wts, inc=d_ws, inck=16)
                DWS_LD[r] = d_ws.n
                em("sync", lambda e, r=r: e.dma_start(
                    out=wdr[:, :, 128 * r:128 * r + 128].transpose([1, 0, 2]),
                    in_=xstage[r % 2]), waits=[(d_ws, DWS_LD[r])],
                   inc=d_ws, inck=16)
                DWS_ST[r] = d_ws.n
            SYM[("DWS_REP", 0)] = (d_ws, d_ws.n)
            return d_ws.n

        DXL_S = {}

        def front_dma(sg, pre_repack=False):
            if sg > 0:
                for t in range(4):
                    wts = [("CS_QH", (sg - 1, 11))] if t == 0 else []
                    em("sync", lambda e, sg=sg, t=t: e.dma_start(
                        out=xp_buf[:, t, :],
                        in_=xprime.ap()[512 * sg + 128 * t:512 * sg + 128 * (t + 1), :]),
                       waits=wts, inc=d_xl, inck=16)
                    DXL_S[(sg, t)] = d_xl.n

            for fi in range(16):
                r, hb = fi // 2, fi % 2
                wts = [(s_cc, SCC_AG)] if (sg == 0 and fi == 0) else []
                if sg > 0 or fi >= 3:
                    pr, pn = (sg, 2 * (fi - 3) + 1) if fi >= 3 \
                        else (sg - 1, [31, 27, 29][fi])
                    wts.append(("CT_GU", (pr, pn)))
                em("sync", lambda e, r=r, hb=hb, fi=fi: fetch_gu(
                    e, wg_ring[fi % 3], r, hb, OFF_G), waits=wts,
                   inc=d_g[fi % 2], inck=16)
                em("sync", lambda e, r=r, hb=hb, fi=fi: fetch_gu(
                    e, wu_ring[fi % 3], r, hb, OFF_U), inc=d_g[fi % 2], inck=16)
                DWS_GU[(sg, fi)] = (d_g[fi % 2], d_g[fi % 2].n)
                if pre_repack and fi == 2:
                    emit_repack()
            for g in range(16):
                wts = []
                gi = 16 * sg + g
                if sg == 1 and g == 0:
                    wts.append(("DWS_REP", 0))
                if gi >= 3:
                    pg = gi - 3
                    wts.append(("CT_DW", (pg // 16, pg % 16)))
                if sg == 0:
                    for s in range(8):
                        em("gpsimd", lambda e, g=g, s=s, gi=gi: e.dma_start(
                            out=wd_ring[gi % 3][:, :, 128 * s:128 * s + 128],
                            in_=agd[s][2 * g:2 * g + 2].transpose([1, 0, 2])),
                           waits=(wts + [(s_cc, SCC_AG)]) if s == 0 else [],
                           inc=d_d[gi % 3], inck=16)
                else:
                    em("gpsimd", lambda e, sg=sg, g=g, gi=gi: fetch_wd(
                        e, wd_ring[gi % 3], g), waits=wts, inc=d_d[gi % 3], inck=16)
                DWS_D[(sg, g)] = (d_d[gi % 3], d_d[gi % 3].n)
            for b in range(4):
                wts = [(d_xp, 16 * (4 * sg + b + 1))]
                prev = 4 * sg + b - 1
                if prev >= 1:
                    wts.append((d_y, 16 * prev))
                em("gpsimd", lambda e, sg=sg, b=b: e.dma_start(
                    out=res_ring[b % 2],
                    in_=xprime.ap()[512 * sg + 128 * b:512 * sg + 128 * (b + 1), :]),
                   waits=wts, inc=d_rs, inck=16)
                DRS[(sg, b)] = d_rs.n

        CV_LN2S = {0: None}

        def front_stats(sg):
            CV_LN2S[sg] = emit_ln2_stats(
                [[(d_xl, DXL_S[(sg, t)])] for t in range(4)])

        def front_rest(sg):
            if sg > 0 and sg not in CV_LN2S:
                front_stats(sg)
            CV_LN2 = CV_LN2_0 if sg == 0 else CV_LN2S[sg]

            def ln2rstd(s):
                s.activation(st2_b[:], st2_rstd[:], Act.Ln, bias=cst[:, 0:1],
                             scale=cst[:, 4:5])
                return s.activation(st2_rstd[:], st2_b[:], Act.Exp, bias=cst[:, 5:6],
                                    scale=cst[:, 2:3])
            em("scalar", ln2rstd, waits=[(c_v, CV_LN2)], inc=c_s)
            CS_R2 = c_s.n
            em("vector", lambda e: e.scalar_tensor_tensor(
                st2_lnb[:], st2_a[:], -1.0, st2_rstd[:], Alu.mult, Alu.mult),
               waits=[(c_s, CS_R2)])
            em("vector", lambda e: SEP(e), inc=c_v)
            CV_LNB2 = c_v.n

            def chain(v, k):
                slot = x2s[k % 4]
                v.tensor_scalar(st_c[:, k:k + 1], st_mx2[:, k:k + 1], EPS, None, Alu.max)
                SEP(v)
                v.reciprocal(st_rc[:, k:k + 1], st_c[:, k:k + 1])
                v.tensor_mul(st_c3a[:, k:k + 1], st_c[:, k:k + 1], st_c[:, k:k + 1])
                v.tensor_scalar(scrV2[:], slot, 127.0, None, Alu.mult)
                v.tensor_mul(st_s0[:, k:k + 1], st_c3a[:, k:k + 1], st_c[:, k:k + 1])
                v.tensor_scalar(scrV2[:], scrV2[:], st_rc[:, k:k + 1], MAGIC,
                                Alu.mult, Alu.add)
                v.tensor_scalar(st_F[:, k:k + 1], st_s0[:, k:k + 1],
                                kconsts[:, 3:4], None, Alu.mult)
                return v.tensor_scalar(qx_slot(k), scrV2[:], MAGIC, None, Alu.subtract)

            def emit_qxT(k):
                wts = [("CV_QX", (sg, k))]
                if k == 0 and sg > 0:
                    wts.append(("CV_OUT0", sg - 1))
                if k > 0:
                    wts.append(("CV_QXEV", (sg, k - 1)))
                for m in range(8):
                    em("tensor", lambda e, k=k, m=m: e.matmul(
                        psT[:, 128 * m:128 * m + 128],
                        qx_slot(k)[:, 128 * m:128 * m + 128],
                        identb[:], start=True, stop=True),
                       waits=wts if m == 0 else [], inc=c_t if m == 7 else None)
                SYM[("CT_QXT", sg, k)] = (c_t, c_t.n)
                ctv = c_t.n

                def do_evac():
                    em("vector", lambda e, k=k: e.tensor_copy(
                        qxT[:, :, 128 * k:128 * k + 128],
                        psT[:, :].rearrange("p (m o) -> p m o", o=128)),
                       waits=[(c_t, ctv)], inc=c_v)
                    CV_QXEV[(sg, k)] = c_v.n
                    SYM[("CV_QXEV", sg, k)] = (c_v, c_v.n)
                if sg == 0:
                    do_evac()
                else:
                    FRONT_EVACS.append(do_evac)

            for jj in range(4):
                wts = [(c_v, CV_LNB2)]
                if sg >= 1:
                    wts.append(("CT_QXT", (sg - 1, jj)))
                em("scalar", lambda e, jj=jj: e.activation(
                    x2s[jj % 4], xp_buf[:, jj, :], Act.Identity,
                    scale=st2_rstd[:, jj:jj + 1], bias=st2_lnb[:, jj:jj + 1]),
                   waits=wts, inc=c_s)
                cs_x2 = c_s.n
                em("vector", lambda e, jj=jj: e.reduce_max(
                    st_mx2[:, jj:jj + 1], x2s[jj % 4], AX, apply_absolute_value=True),
                   waits=[(c_s, cs_x2)])
                if jj >= 1:
                    em("vector", lambda e, k=jj - 1: chain(e, k), inc=c_v)
                    SYM[("CV_QX", sg, jj - 1)] = (c_v, c_v.n)
                    emit_qxT(jj - 1)
            em("vector", lambda e: chain(e, 3), inc=c_v)
            SYM[("CV_QX", sg, 3)] = (c_v, c_v.n)
            emit_qxT(3)

        def emit_front(sg, pre_repack=False):
            front_dma(sg, pre_repack)
            if sg > 0:
                front_stats(sg)
            front_rest(sg)

        def emit_mid(sg):
            for n in range(32):
                fi, ol = n // 2, n % 2
                gbank = psA[:, 0:512] if n % 2 == 0 else psB[:, 0:512]
                ubank = psA[:, 512:1024] if n % 2 == 0 else psB[:, 512:1024]
                wts = [DWS_GU[(sg, fi)]]
                if n == 0:
                    wts.append((c_v, CV_QXEV[(sg, 3)]))
                if n == 12 and sg > 0:
                    wts.append((d_y, 64 * sg))
                if n >= 2:
                    wts += [("CS_RG", (sg, n - 2)), ("CV_A", (sg, n - 2))]
                elif sg > 0:
                    wts.append((c_v, CV_OUT[(sg - 1, 3)]))
                for a in range(8):
                    em("tensor", lambda e, a=a, fi=fi, ol=ol, gbank=gbank: e.matmul(
                        gbank, wg_ring[fi % 3][:, a, 128 * ol:128 * ol + 128],
                        qxT[:, a, :], start=(a == 0), stop=(a == 7)),
                       waits=wts if a == 0 else [])
                for a in range(8):
                    em("tensor", lambda e, a=a, fi=fi, ol=ol, ubank=ubank: e.matmul(
                        ubank, wu_ring[fi % 3][:, a, 128 * ol:128 * ol + 128],
                        qxT[:, a, :], start=(a == 0), stop=(a == 7)),
                       inc=c_t if a == 7 else None)
                CT_GU[(sg, n)] = c_t.n
                SYM[("CT_GU", sg, n)] = (c_t, c_t.n)

                em("scalar", lambda e, n=n, gbank=gbank: e.activation(
                    rg_sb[n % 2][:], gbank, Act.Relu),
                   waits=[(c_t, CT_GU[(sg, n)])], inc=c_s)
                SYM[("CS_RG", sg, n)] = (c_s, c_s.n)
                em("vector", lambda e, n=n, ubank=ubank: e.scalar_tensor_tensor(
                    A_sb[:], ubank, 1.0, rg_sb[n % 2][:], Alu.mult, Alu.mult),
                   waits=[("CS_RG", (sg, n))], inc=c_v)
                SYM[("CV_A", sg, n)] = (c_v, c_v.n)
                hw = [(d_xp, 16 * (n // 2 + 1))] if sg == 0 else []
                em("vector", lambda e, n=n: e.tensor_mul(h_sb[:, n, :], rg_sb[n % 2][:],
                                                         A_sb[:]), waits=hw, inc=c_v)
                if n == 0:
                    em("vector", lambda e: e.tensor_scalar(
                        mxh[:], h_sb[:, 0, :], -1.0, None, Alu.mult))
                else:
                    em("vector", lambda e, n=n: e.scalar_tensor_tensor(
                        mxh[:], h_sb[:, n, :], -1.0, mxh[:], Alu.mult, Alu.max))
                em("vector", lambda e, n=n: e.tensor_tensor(
                    mxh[:], mxh[:], h_sb[:, n, :], Alu.max), inc=c_v)
                if n == 31:
                    SYM[("CV_MIDV", sg)] = (c_v, c_v.n)

        def emit_tok(sg):
            em("vector", lambda e: SEP(e), inc=c_v)
            CV_HLAST = c_v.n
            for b in range(4):
                em("tensor", lambda e, b=b: e.matmul(
                    psT[:, 128 * b:128 * b + 128], mxh[:, 128 * b:128 * b + 128],
                    identf[:], start=True, stop=True),
                   waits=[(c_v, CV_HLAST)] if b == 0 else [],
                   inc=c_t if b == 3 else None)
            CT_MXT = c_t.n

            def tokmath(v):
                for b in range(4):
                    v.reduce_max(st_mxh[:, b:b + 1], psT[:, 128 * b:128 * b + 128], AX)
                SEP(v)
                v.tensor_mul(st_m1[:], st_mxh[:], st_F[:])
                SEP(v)
                v.tensor_scalar(st_m2[:], st_m1[:], EPS, None, Alu.max)
                SEP(v)
                v.reciprocal(st_rc2[:], st_m2[:])
                v.tensor_scalar(st_fd[:], st_m2[:], kconsts[:, 4:5], None, Alu.mult)
                SEP(v)
                v.tensor_mul(st_s0[:], st_rc2[:], st_F[:])
                SEP(v)
                v.tensor_scalar(st_sint[:], st_s0[:], 127.0, None, Alu.mult)
                return SEP(v)
            em("vector", tokmath, waits=[(c_t, CT_MXT)], inc=c_v)
            CV_TOK = c_v.n
            for b in range(4):
                em("tensor", lambda e, b=b: e.matmul(
                    psT[0:1, 128 * b:128 * b + 128], st_sint[:, b:b + 1], identf[:],
                    start=True, stop=True),
                   waits=[(c_v, CV_TOK)] if b == 0 else [], inc=c_t if b == 3 else None)
            CT_ST = c_t.n
            em("vector", lambda e: e.tensor_copy(sseg_row[:, :], psT[0:1, 0:512]),
               waits=[(c_t, CT_ST)])
            em("vector", lambda e: SEP(e), inc=c_v)
            CV_SSEG = c_v.n
            em("tensor", lambda e: e.matmul(
                psT[0:P, 0:512], onesf1[0:1, :], sseg_row[0:1, :],
                start=True, stop=True),
               waits=[(c_v, CV_SSEG)], inc=c_t)
            CT_SBB = c_t.n
            em("vector", lambda e: e.tensor_copy(s_b[:], psT[:, 0:512]),
               waits=[(c_t, CT_SBB)], inc=c_v)
            return c_v.n

        def emit_qh(sg, CV_SBV, v_hook=None):
            CS_QH = {}
            for n in range(32):
                if n == 18 and v_hook is not None:
                    v_hook()
                wts = [(c_v, CV_SBV)]
                if n >= 2:
                    wts.append(("CS_QH", (sg, n - 2)))
                em("vector", lambda e, n=n: e.tensor_mul(qh0[n % 2][:], h_sb[:, n, :],
                                                         s_b[:]),
                   waits=wts, inc=c_v)
                cv_q0 = c_v.n
                em("scalar", lambda e, n=n: e.activation(
                    scrS3[:], qh0[n % 2][:], Act.Identity, bias=cst[:, 1:2],
                    scale=cst[:, 4:5]),
                   waits=[(c_v, cv_q0)])
                qw = []
                if n >= 8:
                    qw.append(("CT_DW", (sg, (n - 8) // 2)))
                elif sg >= 1:
                    qw.append(("CT_DW", (sg - 1, (24 + n) // 2)))
                em("scalar", lambda e, n=n: e.activation(
                    qh_sb[:, n % 8, :], scrS3[:], Act.Identity, bias=cst[:, 6:7],
                    scale=cst[:, 4:5]), waits=qw, inc=c_s)
                CS_QH[n] = c_s.n
                SYM[("CS_QH", sg, n)] = (c_s, c_s.n)
            return CS_QH

        def emit_down(sg):
            bank_of = [psT[:, 0:512], psT[:, 512:1024], psC[:, 0:512], psC[:, 512:1024],
                       psA[:, 0:512], psA[:, 512:1024], psB[:, 0:512], psB[:, 512:1024]]
            for i2 in range(32):
                g = i2 // 2
                wts = [("CS_QH", (sg, i2))]
                if i2 == 0:
                    wts.append(DWS_D[(sg, 0)])
                    if sg > 0:
                        wts.append((c_v, CV_OUT[(sg - 1, 3)]))
                elif i2 % 2 == 0:
                    wts.append(DWS_D[(sg, g)])
                gi = 16 * sg + g
                for b in range(4):
                    for half in range(2):
                        em("tensor", lambda e, i2=i2, b=b, half=half, gi=gi: e.matmul(
                            bank_of[2 * b + half],
                            qh_sb[:, i2 % 8, 128 * b:128 * b + 128],
                            wd_ring[gi % 3][:, i2 % 2, 512 * half:512 * half + 512],
                            start=(i2 == 0), stop=(i2 == 31)),
                           waits=wts if (b == 0 and half == 0) else [],
                           inc=(c_t if (i2 % 2 == 1 and b == 3 and half == 1) else None))
                if i2 % 2 == 1:
                    SYM[("CT_DW", sg, g)] = (c_t, c_t.n)
            CT_DONE[sg] = c_t.n
            SYM[("CT_DONE", sg)] = (c_t, c_t.n)
            return bank_of

        def emit_evac(sg, bank_of):
            for b in range(4):
                def devac(v, sg=sg, b=b):
                    v.scalar_tensor_tensor(out_sb[b % 4][:, 0:512], bank_of[2 * b],
                                           st_fd[:, b:b + 1], res_ring[b % 2][:, 0:512],
                                           Alu.mult, Alu.add)
                    return v.scalar_tensor_tensor(out_sb[b % 4][:, 512:1024],
                                                  bank_of[2 * b + 1],
                                                  st_fd[:, b:b + 1],
                                                  res_ring[b % 2][:, 512:1024],
                                                  Alu.mult, Alu.add)
                wts = [(c_t, CT_DONE[sg]), (d_rs, DRS[(sg, b)])]
                if sg > 0:
                    wts.append((d_y, 16 * (4 * sg + b - 3)))
                em("vector", devac, waits=wts)
                em("vector", lambda e: SEP(e), inc=c_v)
                CV_OUT[(sg, b)] = c_v.n
                if b == 0:
                    SYM[("CV_OUT0", sg)] = (c_v, c_v.n)
                if 1 <= b < len(FRONT_EVACS) + 1:
                    FRONT_EVACS[b - 1]()
            em("vector", lambda e: SEP(e), inc=c_v)
            cv_tail = c_v.n
            if len(FRONT_EVACS) == 4:
                FRONT_EVACS[3]()
            FRONT_EVACS.clear()
            for b in range(4):
                j = 4 * sg + b
                em("scalar", lambda e, j=j, b=b: e.dma_start(
                    out=y_out[j * P:(j + 1) * P, :], in_=out_sb[b % 4][:]),
                   waits=[(c_v, CV_OUT[(sg, b + 1)] if b < 3 else cv_tail)],
                   inc=d_y, inck=16)

        emit_front(0)
        for sg in range(NSUP):
            emit_mid(sg)
            cv_sbv = emit_tok(sg)
            nxt = sg + 1
            if nxt < NSUP:
                front_dma(nxt, pre_repack=(sg == 0))
                emit_qh(sg, cv_sbv, v_hook=lambda nxt=nxt: front_stats(nxt))
                bank_of = emit_down(sg)
                front_rest(nxt)
            else:
                emit_qh(sg, cv_sbv)
                bank_of = emit_down(sg)
            emit_evac(sg, bank_of)

        em("gpsimd", lambda e: e.engine_nop(), waits=[(d_y, 16 * NT)])

        resolve_all()

        with nc.Block() as block:
            @block.sync
            def _(e):
                W.run("sync", e)

            @block.gpsimd
            def _(e):
                W.run("gpsimd", e)

            @block.tensor
            def _(e):
                W.run("tensor", e)

            @block.vector
            def _(e):
                W.run("vector", e)

            @block.scalar
            def _(e):
                W.run("scalar", e)

    return nc


_NC_CACHE = None


def kernel(L0_attn_x, L0_attn_Wq, L0_attn_Wk, L0_attn_Wv,
           L0_ffn_w_gate, L0_ffn_w_up, L0_ffn_w_down):
    global _NC_CACHE
    x = np.ascontiguousarray(np.asarray(L0_attn_x, dtype=np.float32))
    wq = np.ascontiguousarray(np.asarray(L0_attn_Wq, dtype=np.float32))
    wk = np.ascontiguousarray(np.asarray(L0_attn_Wk, dtype=np.float32))
    wv = np.ascontiguousarray(np.asarray(L0_attn_Wv, dtype=np.float32))
    wg = np.ascontiguousarray(np.asarray(L0_ffn_w_gate, dtype=np.float32))
    wu = np.ascontiguousarray(np.asarray(L0_ffn_w_up, dtype=np.float32))
    wd = np.ascontiguousarray(np.asarray(L0_ffn_w_down, dtype=np.float32))

    in_maps = []
    for i in range(NCORES):
        xi = np.ascontiguousarray(x[:, TPC * i:TPC * (i + 1), :]).reshape(R, E)
        in_maps.append({
            "x": xi, "wq": wq, "wk": wk, "wv": wv,
            "wg_sh": np.ascontiguousarray(wg[512 * i:512 * (i + 1)]),
            "wu_sh": np.ascontiguousarray(wu[512 * i:512 * (i + 1)]),
            "wd_sh": np.ascontiguousarray(wd[128 * i:128 * (i + 1)]),
        })
    if _NC_CACHE is None:
        _NC_CACHE = build_nc()
    res = run_bass_kernel_spmd(_NC_CACHE, in_maps, core_ids=list(range(NCORES)))
    out = np.empty((C, 2048, E), dtype=np.float32)
    for i in range(NCORES):
        out[:, TPC * i:TPC * (i + 1), :] = res.results[i]["y"].reshape(C, TPC, E)
    return out


# version 34 (final)
# speedup vs baseline: 1.0336x; 1.0267x over previous
import contextlib
import numpy as np

import concourse.bass as bass
import concourse.mybir as mybir
from concourse.bass_utils import run_bass_kernel_spmd

F32 = mybir.dt.float32
BF16 = mybir.dt.bfloat16
Alu = mybir.AluOpType
Act = mybir.ActivationFunctionType
AX = mybir.AxisListType.X

NCORES = 8
P = 128
C = 8
TPC = 256
R = 2048
NT = 16
E = 1024
I = 4096
NSUP = 4
EPS = 1e-5
MAGIC = 12582912.0
W_COUNT = 4096.0 * 1024.0

OFF_G = 0
OFF_U = 512 * 1024
OFF_D = 1024 * 1024
SHARD = 3 * 512 * 1024


class Sem:
    def __init__(self, handle):
        self.h = handle
        self.n = 0


class Walk:
    def __init__(self):
        self.ops = {k: [] for k in ("sync", "tensor", "vector", "scalar", "gpsimd")}

    def emit(self, eng, fn, waits=(), inc=None, inck=1):
        if inc is not None:
            inc.n += inck
        self.ops[eng].append((fn, tuple(waits), inc, inck))

    def run(self, eng_name, eng):
        for fn, waits, inc, inck in self.ops[eng_name]:
            for s, v in waits:
                eng.wait_ge(s.h, v)
            ins = fn(eng)
            if inc is not None:
                ins.then_inc(inc.h, inck)


def build_nc():
    nc = bass.Bass("TRN2", target_bir_lowering=False, debug=False, detect_race_conditions=False)

    x_in = nc.dram_tensor("x", [R, E], F32, kind="ExternalInput").ap()
    wq_in = nc.dram_tensor("wq", [64, 64], F32, kind="ExternalInput").ap()
    wk_in = nc.dram_tensor("wk", [64, 64], F32, kind="ExternalInput").ap()
    wv_in = nc.dram_tensor("wv", [64, 64], F32, kind="ExternalInput").ap()
    wg_in = nc.dram_tensor("wg_sh", [512, E], F32, kind="ExternalInput").ap()
    wu_in = nc.dram_tensor("wu_sh", [512, E], F32, kind="ExternalInput").ap()
    wd_in = nc.dram_tensor("wd_sh", [128, I], F32, kind="ExternalInput").ap()
    y_out = nc.dram_tensor("y", [R, E], F32, kind="ExternalOutput").ap()

    cc_ar_in = nc.dram_tensor("cc_ar_in", [67, 128], F32)
    cc_ar_out = nc.dram_tensor("cc_ar_out", [67, 128], F32)
    cc_ag_in = nc.dram_tensor("cc_ag_in", [SHARD], BF16)
    cc_ag_out = nc.dram_tensor("cc_ag_out", [NCORES * SHARD], BF16, addr_space="Shared")
    xprime = nc.dram_tensor("xprime", [R, E], F32)
    vsum_dram = nc.dram_tensor("vsum_dram", [C, E], F32)
    wd_repack = nc.dram_tensor("wd_repack", [I, E], BF16)

    with contextlib.ExitStack() as st:
        def sb(name, shape, dtype=F32):
            return st.enter_context(nc.sbuf_tensor(name, shape, dtype))

        def psm(name, shape, dtype=F32):
            return st.enter_context(nc.psum_tensor(name, shape, dtype))

        identf = sb("identf", [P, P], F32)
        identb = sb("identb", [P, P], BF16)
        onesf1 = sb("onesf1", [1, P], F32)
        ones1b = sb("ones1b", [P, 1], BF16)
        cst = sb("cst", [P, 8], F32)
        wqkv = sb("wqkv", [64, 192], F32)
        wqkv_b = sb("wqkv_b", [64, 192], BF16)
        wqkv_t = sb("wqkv_t", [64, 192], BF16)
        BW = sb("BW", [P, 3, P], BF16)
        kconsts = sb("kconsts", [P, 8], F32)
        scal3 = sb("scal3", [3, P], F32)
        srow = sb("srow", [1, 8], F32)
        absn = sb("absn", [P, 12], F32)
        st_sum = sb("st_sum", [P, NT], F32)
        st_ssq = sb("st_ssq", [P, NT], F32)
        st_mean = sb("st_mean", [P, NT], F32)
        st_var = sb("st_var", [P, NT], F32)
        st_rstd = sb("st_rstd", [P, NT], F32)
        st_lnb = sb("st_lnb", [P, NT], F32)
        st_s = sb("st_s", [P, NT * 16], F32)
        st_e = sb("st_e", [P, NT * 16], F32)
        st_p = sb("st_p", [P, NT * 16], F32)
        sm_mx = sb("sm_mx", [P, 32], F32)
        sm_sum = sb("sm_sum", [P, 32], F32)
        sm_r = sb("sm_r", [P, 32], F32)
        se2 = sb("se2", [P, NT * 16], F32)
        xsum_sb = sb("xsum_sb", [C, E], F32)
        xsum_all = sb("xsum_all", [C, E], F32)
        xsumT = sb("xsumT", [P, 64], BF16)
        vsum_sb = sb("vsum_sb", [C, E], F32)
        ind3 = sb("ind3", [P, 8, 8], BF16)
        st2_sum = sb("st2_sum", [P, 4], F32)
        st2_ssq = sb("st2_ssq", [P, 4], F32)
        st2_a = sb("st2_a", [P, 4], F32)
        st2_b = sb("st2_b", [P, 4], F32)
        st2_rstd = sb("st2_rstd", [P, 4], F32)
        st2_lnb = sb("st2_lnb", [P, 4], F32)
        st_mx2 = sb("st_mx2", [P, 4], F32)
        st_c = sb("st_c", [P, 4], F32)
        st_rc = sb("st_rc", [P, 4], F32)
        st_s127 = sb("st_s127", [P, 4], F32)
        st_c3a = sb("st_c3a", [P, 4], F32)
        st_F = sb("st_F", [P, 4], F32)
        st_mxh = sb("st_mxh", [P, 4], F32)
        st_m1 = sb("st_m1", [P, 4], F32)
        st_m2 = sb("st_m2", [P, 4], F32)
        st_rc2 = sb("st_rc2", [P, 4], F32)
        st_s0 = sb("st_s0", [P, 4], F32)
        st_sint = sb("st_sint", [P, 4], F32)
        st_fd = sb("st_fd", [P, 4], F32)
        sseg_row = sb("sseg_row", [1, 512], F32)
        mxh = sb("mxh", [P, 512], F32)
        s_b = sb("s_b", [P, 512], F32)
        sepb = sb("sepb", [P, 512], F32)

        AW = 45232
        arena = sb("arena", [P, AW], F32)

        class Alloc:
            def __init__(self):
                self.off = 0

            def f32(self, words):
                o = self.off
                assert o + words <= AW, f"arena overflow {o}+{words}"
                self.off = o + words
                return arena[:, o:o + words]

            def bf16(self, elems):
                assert elems % 2 == 0
                return self.f32(elems // 2).bitcast(BF16)

        aa = Alloc()
        xr = [aa.f32(E) for _ in range(NT)]
        wshg_f = aa.f32(4 * E)
        wshg = wshg_f.rearrange("p (t i) -> p t i", i=E)
        wshu_f = aa.f32(4 * E)
        wshu = wshu_f.rearrange("p (t i) -> p t i", i=E)
        wshd = aa.f32(I)
        qshg = aa.bf16(4 * E).rearrange("p (t i) -> p t i", i=E)
        qshu = aa.bf16(4 * E).rearrange("p (t i) -> p t i", i=E)
        qshd = aa.bf16(I)
        qwgT = aa.bf16(8 * 512).rearrange("p (m o) -> p m o", o=512)
        qwuT = aa.bf16(8 * 512).rearrange("p (m o) -> p m o", o=512)
        qwdT = aa.bf16(32 * 128).rearrange("p (m o) -> p m o", o=128)
        scrS = aa.f32(E)
        x1buf = [scrS[:, 0:512].bitcast(BF16), scrS[:, 512:1024].bitcast(BF16)]
        x1T = [aa.bf16(E) for _ in range(2)]
        q_sb = aa.f32(E)
        scrV = aa.f32(E)
        qk_scr = scrV
        assert aa.off <= AW, aa.off

        fa = Alloc()
        h_raw = fa.f32(32 * 512)
        h_sb = h_raw.rearrange("p (n o) -> p n o", o=512)
        xpraw = h_raw[:, 0:4096]
        xp_buf = xpraw.rearrange("p (t i) -> p t i", i=E)
        qh_raw = fa.f32(8 * 256)
        qh_sb = qh_raw.bitcast(BF16).rearrange("p (n o) -> p n o", o=512)
        xst_raw = fa.f32(4096)
        xstage = [xst_raw[:, 0:2048].bitcast(BF16).rearrange("p (m o) -> p m o", o=128),
                  xst_raw[:, 2048:4096].bitcast(BF16).rearrange("p (m o) -> p m o", o=128)]
        qxT = fa.bf16(8 * 512).rearrange("p (m o) -> p m o", o=512)
        wg_ring = [fa.bf16(8 * 256).rearrange("p (m o) -> p m o", o=256) for _ in range(3)]
        wu_ring = [fa.bf16(8 * 256).rearrange("p (m o) -> p m o", o=256) for _ in range(3)]
        wd_ring = [fa.bf16(2 * 1024).rearrange("p (m o) -> p m o", o=1024) for _ in range(3)]
        scr_x2 = fa.f32(E)
        x2b = fa.f32(E)
        scrV2 = fa.f32(E)
        rgraw = fa.f32(1024)
        rg_sb = [rgraw[:, 0:512], rgraw[:, 512:1024]]
        A_sb = fa.f32(512)
        qh0raw = fa.f32(1024)
        x2c = rgraw
        x2d = qh0raw
        qh0 = [qh0raw[:, 0:512], qh0raw[:, 512:1024]]
        scrS3 = fa.f32(512)
        out_sb = [fa.f32(E) for _ in range(2)]
        out_sb.append(h_raw[:, 6144:7168])
        out_sb.append(h_raw[:, 7168:8192])
        res_ring = [fa.f32(E) for _ in range(2)]
        x2s = [scr_x2, x2b, x2c, x2d]
        qxs = [s[:, 0:512].bitcast(BF16) for s in x2s]
        assert fa.off <= AW, fa.off

        def qx_slot(jj):
            return qxs[jj % 4]

        psT = psm("psT", [P, 1024], F32)
        psA = psm("psA", [P, 1024], F32)
        psB = psm("psB", [P, 1024], F32)
        psC = psm("psC", [P, 1024], F32)

        names = ("d_x", "d_w", "d_cc", "d_cb", "d_xp", "d_xl", "d_ws", "d_rs", "d_y", "d_v",
                 "d_b0", "d_b1", "d_g0", "d_g1", "d_d0", "d_d1", "d_d2",
                 "c_t", "c_v", "c_s", "c_g", "s_cc")
        S = {k: Sem(st.enter_context(nc.semaphore(k))) for k in names}
        d_x, d_w, d_cc, d_cb, d_xp, d_xl, d_ws, d_rs, d_y, d_v = (S[k] for k in
            ("d_x", "d_w", "d_cc", "d_cb", "d_xp", "d_xl", "d_ws", "d_rs", "d_y", "d_v"))
        d_b = [S["d_b0"], S["d_b1"]]
        d_g = [S["d_g0"], S["d_g1"]]
        d_d = [S["d_d0"], S["d_d1"], S["d_d2"]]
        c_t, c_v, c_s, c_g, s_cc = (S[k] for k in ("c_t", "c_v", "c_s", "c_g", "s_cc"))

        W = Walk()
        em = W.emit

        def SEP(v):
            return v.tensor_copy(sepb[:, 0:512], mxh[:, 0:512])

        SYM = {}

        def resolve_all():
            for eng in W.ops:
                out = []
                for fn, waits, inc, inck in W.ops[eng]:
                    nw = []
                    for s, v in waits:
                        if isinstance(s, str):
                            sem, val = SYM[(s,) + (v if isinstance(v, tuple) else (v,))]
                            nw.append((sem, val))
                        else:
                            nw.append((s, v))
                    out.append((fn, tuple(nw), inc, inck))
                W.ops[eng] = out

        def mark(tag, *key):
            pass

        em("sync", lambda e: e.dma_start(out=wqkv[:, 0:64], in_=wq_in), inc=d_w, inck=16)
        em("sync", lambda e: e.dma_start(out=wqkv[:, 64:128], in_=wk_in), inc=d_w, inck=16)
        em("sync", lambda e: e.dma_start(out=wqkv[:, 128:192], in_=wv_in), inc=d_w, inck=16)
        for j in range(NT):
            em("sync", lambda e, j=j: e.dma_start(
                out=xr[j], in_=x_in[j * P:(j + 1) * P, :]), inc=d_x, inck=16)
        em("sync", lambda e: e.dma_start(
            out=wshg, in_=wg_in.rearrange("(t p) i -> p t i", p=P)), inc=d_w, inck=16)
        em("sync", lambda e: e.dma_start(
            out=wshu, in_=wu_in.rearrange("(t p) i -> p t i", p=P)), inc=d_w, inck=16)
        em("sync", lambda e: e.dma_start(out=wshd, in_=wd_in), inc=d_w, inck=16)
        DW_IN = d_w.n

        def vconsts(v):
            v.memset(sepb[:], 0.0)
            v.memset(mxh[:], 0.0)
            v.memset(identf[:], 0.0)
            v.memset(onesf1[:], 1.0)
            v.memset(ones1b[:], 1.0)
            v.memset(cst[:, 0:1], EPS)
            v.memset(cst[:, 1:2], MAGIC)
            v.memset(cst[:, 2:3], -0.5)
            v.memset(cst[:, 3:4], 1.0 / 32.0)
            v.memset(cst[:, 4:5], 1.0)
            v.memset(cst[:, 5:6], 0.0)
            v.memset(cst[:, 6:7], -MAGIC)
            v.memset(BW[:], 0.0)
            v.memset(ind3[:], 0.0)
            v.memset(scal3[:], 0.0)
            v.memset(srow[:], 0.0)
            SEP(v)
            for _c in range(8):
                v.memset(ind3[:, _c, _c:_c + 1], 1.0)
            return SEP(v)
        em("vector", vconsts, inc=c_v)
        CV_CONST = c_v.n

        def gconsts(g):
            g.wait_ge(c_v.h, CV_CONST)
            return g.affine_select(out=identf[:], in_=identf[:],
                                   compare_op=Alu.not_equal, fill=1.0, base=0,
                                   pattern=[[-1, P]], channel_multiplier=1)
        em("gpsimd", gconsts, inc=c_g)
        CG = c_g.n

        em("vector", lambda e: e.tensor_copy(identb[:], identf[:]),
           waits=[(c_g, CG)])
        em("vector", lambda e: SEP(e), inc=c_v)
        CV_IDB = c_v.n
        em("vector", lambda e: e.tensor_copy(wqkv_b[:], wqkv[:]),
           waits=[(d_w, DW_IN)], inc=c_v)
        CV_WQKVB = c_v.n
        for w in range(3):
            em("tensor", lambda e, w=w: e.matmul(
                psT[0:64, 64 * w:64 * w + 64], wqkv_b[:, 64 * w:64 * w + 64],
                identb[0:64, 0:64], start=True, stop=True),
               waits=[(c_v, CV_WQKVB)] if w == 0 else [],
               inc=c_t if w == 2 else None)
        CT_WQT = c_t.n
        em("vector", lambda e: e.tensor_copy(wqkv_t[:, :], psT[0:64, 0:192]),
           waits=[(c_t, CT_WQT)])
        em("vector", lambda e: SEP(e), inc=c_v)
        CV_WQT = c_v.n
        for w in range(3):
            em("sync", lambda e, w=w: e.dma_start(
                out=BW[0:64, w, 0:64], in_=wqkv_t[:, 64 * w:64 * w + 64]),
               waits=[(c_v, CV_WQT)] if w == 0 else [], inc=d_w, inck=16)
            em("sync", lambda e, w=w: e.dma_start(
                out=BW[64:128, w, 64:128], in_=wqkv_t[:, 64 * w:64 * w + 64]),
               inc=d_w, inck=16)
        DW_BW = d_w.n

        CV_LNB1G = [0] * 4
        for g4 in range(4):
            sl = slice(4 * g4, 4 * g4 + 4)
            for j in range(4 * g4, 4 * g4 + 4):
                em("vector", lambda e, j=j: e.scalar_tensor_tensor(
                    scrV[:], xr[j], 1.0, xr[j], Alu.mult, Alu.mult,
                    accum_out=st_ssq[:, j:j + 1]),
                   waits=[(d_x, 16 * (4 * g4 + 4))] if j % 4 == 0 else [])
                em("vector", lambda e, j=j: e.tensor_scalar(
                    scrV[:], xr[j], 1.0, 0.0, Alu.mult, Alu.add,
                    accum_out=st_sum[:, j:j + 1]))

            def ln1math(v, sl=sl):
                SEP(v)
                v.tensor_scalar(st_mean[:, sl], st_sum[:, sl], 1.0 / E, None, Alu.mult)
                v.tensor_mul(st_lnb[:, sl], st_sum[:, sl], st_sum[:, sl])
                v.tensor_scalar(st_var[:, sl], st_ssq[:, sl], 1.0 / E, None, Alu.mult)
                SEP(v)
                v.tensor_scalar(st_lnb[:, sl], st_lnb[:, sl], 1.0 / (E * E), None,
                                Alu.mult)
                SEP(v)
                v.tensor_sub(st_var[:, sl], st_var[:, sl], st_lnb[:, sl])
                return SEP(v)
            em("vector", ln1math, inc=c_v)
            CV_LN1V = c_v.n

            def ln1rstd(s, sl=sl):
                s.activation(st_lnb[:, sl], st_var[:, sl], Act.Ln, bias=cst[:, 0:1],
                             scale=cst[:, 4:5])
                return s.activation(st_rstd[:, sl], st_lnb[:, sl], Act.Exp,
                                    bias=cst[:, 5:6], scale=cst[:, 2:3])
            em("scalar", ln1rstd, waits=[(c_v, CV_LN1V)], inc=c_s)
            CS_RSTD1 = c_s.n
            em("vector", lambda e, sl=sl: e.scalar_tensor_tensor(
                st_lnb[:, sl], st_mean[:, sl], -1.0, st_rstd[:, sl],
                Alu.mult, Alu.mult),
               waits=[(c_s, CS_RSTD1)])
            em("vector", lambda e: SEP(e), inc=c_v)
            CV_LNB1G[g4] = c_v.n

        def wabs(v):
            v.reduce_sum(absn[:, 0:1], wshg_f, AX, apply_absolute_value=True)
            v.reduce_sum(absn[:, 1:2], wshu_f, AX, apply_absolute_value=True)
            v.reduce_sum(absn[:, 2:3], wshd[:, :], AX, apply_absolute_value=True)
            return SEP(v)
        em("vector", wabs, waits=[(d_w, DW_IN)], inc=c_v)
        CV_WABS = c_v.n

        CS_X1 = [0] * NT
        CT_TT = [0] * NT
        CT_QK = [0] * NT
        CV_EVX = [0] * NT
        CV_QK = [0] * NT

        for j in range(NT):
            waits = [(c_v, CV_LNB1G[j // 4])]
            if j >= 2:
                waits.append(("CT_QK", j - 2))
            em("scalar", lambda e, j=j: e.activation(
                x1buf[j % 2][:], xr[j], Act.Identity,
                scale=st_rstd[:, j:j + 1], bias=st_lnb[:, j:j + 1]),
               waits=waits, inc=c_s)
            CS_X1[j] = c_s.n
            SYM[("CS_X1", j)] = (c_s, c_s.n)

        for j in range(NT):
            waits = [("CS_X1", j), (c_v, CV_IDB)]
            if j >= 1:
                waits.append(("CV_EVX", j - 1))
            for m in range(8):
                em("tensor", lambda e, j=j, m=m: e.matmul(
                    psT[:, 128 * m:128 * m + 128], x1buf[j % 2][:, 128 * m:128 * m + 128],
                    identb[:], start=True, stop=True),
                   waits=waits if m == 0 else [], inc=c_t if m == 7 else None)
            CT_TT[j] = c_t.n
            SYM[("CT_TT", j)] = (c_t, c_t.n)

            w_q = [(d_w, DW_BW), ("CV_EVX", j)]
            if j >= 1:
                w_q.append(("CV_QK", j - 1))
            for m in range(8):
                em("tensor", lambda e, j=j, m=m: e.matmul(
                    psA[:, 128 * m:128 * m + 128], x1T[j % 2][:, 128 * m:128 * m + 128],
                    BW[:, 0, :], start=True, stop=True),
                   waits=w_q if m == 0 else [])
            for m in range(8):
                em("tensor", lambda e, j=j, m=m: e.matmul(
                    psB[:, 128 * m:128 * m + 128], x1T[j % 2][:, 128 * m:128 * m + 128],
                    BW[:, 1, :], start=True, stop=True))
            c = j // 2
            for half in range(2):
                em("tensor", lambda e, j=j, c=c, half=half: e.matmul(
                    psC[0:8, 512 * half:512 * half + 512], ind3[:, c, :],
                    x1buf[j % 2][:, 512 * half:512 * half + 512],
                    start=(j == 0), stop=(j == NT - 1)),
                   inc=c_t if half == 1 else None)
            CT_QK[j] = c_t.n
            SYM[("CT_QK", j)] = (c_t, c_t.n)

            em("vector", lambda e, j=j: e.tensor_copy(x1T[j % 2][:], psT[:, :]),
               waits=[(c_t, CT_TT[j])], inc=c_v)
            CV_EVX[j] = c_v.n
            SYM[("CV_EVX", j)] = (c_v, c_v.n)
            em("vector", lambda e, j=j: e.tensor_copy(q_sb[:], psA[:, :]),
               waits=[(c_t, CT_QK[j])], inc=c_v)
            em("vector", lambda e, j=j: e.tensor_mul(qk_scr[:], q_sb[:], psB[:, :]))
            em("vector", lambda e, j=j: e.reduce_sum(
                st_s[:, 16 * j:16 * j + 16].rearrange("p (a b) -> p a b", b=1),
                qk_scr[:].rearrange("p (a b) -> p a b", b=64), AX), inc=c_v)
            CV_QK[j] = c_v.n
            SYM[("CV_QK", j)] = (c_v, c_v.n)

        em("tensor", lambda e: e.matmul(psT[0:3, 0:128], absn[:, 0:3], identf[:],
                                        start=True, stop=True),
           waits=[(c_v, CV_WABS), (c_v, CV_EVX[NT - 1])], inc=c_t)
        CT_WPT = c_t.n
        em("vector", lambda e: e.reduce_sum(scal3[0:3, 0:1], psT[0:3, 0:128], AX),
           waits=[(c_t, CT_WPT)], inc=c_v)
        em("vector", lambda e: e.tensor_copy(xsum_sb[:, :], psC[0:8, :]),
           waits=[(c_t, CT_QK[NT - 1])])
        em("vector", lambda e: SEP(e), inc=c_v)
        CV_CCRDY = c_v.n
        em("sync", lambda e: e.dma_start(
            out=cc_ar_in.ap()[0:64, :].rearrange("(c k) f -> c (k f)", k=8),
            in_=xsum_sb[:, :]), waits=[(c_v, CV_CCRDY)], inc=d_cc, inck=16)
        em("sync", lambda e: e.dma_start(out=cc_ar_in.ap()[64:67, :], in_=scal3[:, :]),
           inc=d_cc, inck=16)
        DCC_IN = d_cc.n
        em("gpsimd", lambda e: e.collective_compute(
            "AllReduce", Alu.add, replica_groups=[list(range(NCORES))],
            ins=[cc_ar_in.ap()], outs=[cc_ar_out.ap()]),
           waits=[(d_cc, DCC_IN)], inc=s_cc)
        SCC_AR = s_cc.n
        em("gpsimd", lambda e: e.dma_start(
            out=xsum_all[:, :],
            in_=cc_ar_out.ap()[0:64, :].rearrange("(c k) f -> c (k f)", k=8)),
           waits=[(s_cc, SCC_AR)], inc=d_cb, inck=16)
        em("gpsimd", lambda e: e.dma_start(out=scal3[0:3, 0:128],
                                           in_=cc_ar_out.ap()[64:67, 0:128]),
           inc=d_cb, inck=16)
        DCC_BACK = d_cb.n

        def softmax1(v):
            SEP(v)
            for b in range(2):
                for q4 in range(4):
                    j0, j1 = 4 * q4 + b, 4 * q4 + 2 + b
                    v.tensor_tensor(se2[:, 16 * (2 * q4 + b):16 * (2 * q4 + b) + 16],
                                    st_s[:, 16 * j0:16 * j0 + 16],
                                    st_s[:, 16 * j1:16 * j1 + 16], Alu.max)
            for b in range(2):
                for q2 in range(2):
                    v.tensor_tensor(
                        se2[:, 128 + 16 * (2 * q2 + b):128 + 16 * (2 * q2 + b) + 16],
                        se2[:, 16 * (4 * q2 + b):16 * (4 * q2 + b) + 16],
                        se2[:, 16 * (4 * q2 + 2 + b):16 * (4 * q2 + 2 + b) + 16],
                        Alu.max)
            SEP(v)
            for b in range(2):
                v.tensor_tensor(sm_mx[:, 16 * b:16 * b + 16],
                                se2[:, 128 + 16 * b:128 + 16 * b + 16],
                                se2[:, 128 + 16 * (2 + b):128 + 16 * (2 + b) + 16],
                                Alu.max)
            SEP(v)
            for j in range(NT):
                b = j % 2
                v.tensor_sub(st_e[:, 16 * j:16 * j + 16],
                             st_s[:, 16 * j:16 * j + 16], sm_mx[:, 16 * b:16 * b + 16])
            return SEP(v)
        em("vector", softmax1, waits=[(c_v, CV_QK[NT - 1])], inc=c_v)
        CV_SM1 = c_v.n
        em("scalar", lambda e: e.activation(se2[:], st_e[:], Act.Exp,
                                            bias=cst[:, 5:6], scale=cst[:, 3:4]),
           waits=[(c_v, CV_SM1)], inc=c_s)
        CS_EXP = c_s.n

        def softmax2(v):
            for b in range(2):
                for q4 in range(4):
                    j0, j1 = 4 * q4 + b, 4 * q4 + 2 + b
                    v.tensor_add(st_e[:, 16 * (2 * q4 + b):16 * (2 * q4 + b) + 16],
                                 se2[:, 16 * j0:16 * j0 + 16],
                                 se2[:, 16 * j1:16 * j1 + 16])
            for b in range(2):
                for q2 in range(2):
                    v.tensor_add(
                        st_e[:, 128 + 16 * (2 * q2 + b):128 + 16 * (2 * q2 + b) + 16],
                        st_e[:, 16 * (4 * q2 + b):16 * (4 * q2 + b) + 16],
                        st_e[:, 16 * (4 * q2 + 2 + b):16 * (4 * q2 + 2 + b) + 16])
            SEP(v)
            for b in range(2):
                v.tensor_add(sm_sum[:, 16 * b:16 * b + 16],
                             st_e[:, 128 + 16 * b:128 + 16 * b + 16],
                             st_e[:, 128 + 16 * (2 + b):128 + 16 * (2 + b) + 16])
            SEP(v)
            v.reciprocal(sm_r[:, 0:32], sm_sum[:, 0:32])
            SEP(v)
            for j in range(NT):
                b = j % 2
                v.tensor_mul(st_p[:, 16 * j:16 * j + 16], se2[:, 16 * j:16 * j + 16],
                             sm_r[:, 16 * b:16 * b + 16])
            return SEP(v)
        em("vector", softmax2, waits=[(c_s, CS_EXP)], inc=c_v)
        CV_P = c_v.n

        em("tensor", lambda e: e.matmul(psT[0:1, 0:3], scal3[0:3, 0:1], identf[0:3, 0:3],
                                        start=True, stop=True),
           waits=[(d_cb, DCC_BACK)], inc=c_t)
        CT_S3 = c_t.n

        def smath(v):
            v.tensor_copy(srow[:, 0:3], psT[0:1, 0:3])
            SEP(v)
            v.tensor_scalar(srow[:, 0:3], srow[:, 0:3], 1.0 / W_COUNT, EPS, Alu.mult, Alu.max)
            SEP(v)
            v.tensor_mul(srow[:, 3:4], srow[:, 0:1], srow[:, 0:1])
            v.tensor_scalar(srow[:, 4:5], srow[:, 2:3], 1.0 / 127.0, None, Alu.mult)
            v.reciprocal(srow[:, 5:8], srow[:, 0:3])
            SEP(v)
            v.tensor_mul(srow[:, 3:4], srow[:, 3:4], srow[:, 1:2])
            SEP(v)
            v.tensor_scalar(srow[:, 3:4], srow[:, 3:4], 1.0 / 2048383.0, None, Alu.mult)
            v.tensor_copy(srow[:, 0:1], srow[:, 5:6])
            v.tensor_copy(srow[:, 1:2], srow[:, 6:7])
            v.tensor_copy(srow[:, 2:3], srow[:, 7:8])
            return SEP(v)
        em("vector", smath, waits=[(c_t, CT_S3)], inc=c_v)
        CV_SMATH = c_v.n
        em("tensor", lambda e: e.matmul(psT[0:P, 0:8], onesf1[0:1, :], srow[0:1, 0:8],
                                        start=True, stop=True),
           waits=[(c_v, CV_SMATH)], inc=c_t)
        CT_KC = c_t.n
        em("vector", lambda e: e.tensor_copy(kconsts[:, :], psT[0:P, 0:8]),
           waits=[(c_t, CT_KC)])
        em("vector", lambda e: SEP(e), inc=c_v)
        CV_KC = c_v.n

        qv_done = {}

        def quant(v, wsrc, qdst, col):
            v.tensor_scalar(scrV[:, 0:wsrc.shape[-1]], wsrc,
                            kconsts[:, col:col + 1], MAGIC, Alu.mult, Alu.add)
            v.tensor_scalar(scrV[:, 0:wsrc.shape[-1]], scrV[:, 0:wsrc.shape[-1]],
                            MAGIC, -1.0, Alu.subtract, Alu.max)
            return v.tensor_scalar(qdst, scrV[:, 0:wsrc.shape[-1]], 1.0, None, Alu.min)
        for t in range(4):
            em("vector", lambda e, t=t: quant(e, wshg[:, t, :], qshg[:, t, :], 0),
               waits=[(c_v, CV_KC)] if t == 0 else [], inc=c_v)
            qv_done[("g", t)] = c_v.n
        for t in range(4):
            em("vector", lambda e, t=t: quant(e, wshu[:, t, :], qshu[:, t, :], 1), inc=c_v)
            qv_done[("u", t)] = c_v.n
        for t in range(4):
            em("vector", lambda e, t=t: quant(
                e, wshd[:, 1024 * t:1024 * (t + 1)], qshd[:, 1024 * t:1024 * (t + 1)], 2),
               inc=c_v)
            qv_done[("d", t)] = c_v.n

        CV_WEV = []

        def wtr_batch(which, blocks, evac):
            for i, (lhs, slot) in enumerate(blocks):
                wts = []
                if i == 0:
                    wts = [(c_v, qv_done[which]),
                           (c_v, CV_WEV[-1] if CV_WEV else CV_KC)]
                em("tensor", lambda e, lhs=lhs, slot=slot: e.matmul(
                    psT[:, 128 * slot:128 * slot + 128], lhs, identb[:],
                    start=True, stop=True),
                   waits=wts, inc=c_t if i == len(blocks) - 1 else None)
            ct = c_t.n
            em("vector", lambda e, evac=evac: e.tensor_copy(
                evac, psT[:, :].rearrange("p (m o) -> p m o", o=128)),
               waits=[(c_t, ct)], inc=c_v)
            CV_WEV.append(c_v.n)

        for t in range(4):
            wtr_batch(("g", t), [(qshg[:, t, 128 * m:128 * m + 128], m) for m in range(8)],
                      qwgT[:, :, 128 * t:128 * t + 128])
        for t in range(4):
            wtr_batch(("u", t), [(qshu[:, t, 128 * m:128 * m + 128], m) for m in range(8)],
                      qwuT[:, :, 128 * t:128 * t + 128])
        for t in range(4):
            wtr_batch(("d", t),
                      [(qshd[:, 128 * (8 * t + mm):128 * (8 * t + mm) + 128], mm)
                       for mm in range(8)],
                      qwdT[:, 8 * t:8 * t + 8, :])
        em("vector", lambda e: SEP(e), inc=c_v)
        CV_WEV.append(c_v.n)
        CV_WTR_LAST = CV_WEV[-1]

        bmask = wshu_f[0:16, 0:E]
        indf = wshg_f[0:8, 0:128]

        def bm_init(v):
            v.memset(bmask[:, :], 1.0)
            return v.memset(indf[:, :], 1.0)
        em("vector", bm_init, waits=[(c_v, CV_WTR_LAST)], inc=c_v)
        CV_BM1 = c_v.n

        def bmask_sel(g):
            g.affine_select(out=bmask[:, :], in_=bmask[:, :],
                            compare_op=Alu.is_ge, fill=0.0, base=0,
                            pattern=[[1, E]], channel_multiplier=-64)
            g.affine_select(out=bmask[:, :], in_=bmask[:, :],
                            compare_op=Alu.is_ge, fill=0.0, base=63,
                            pattern=[[-1, E]], channel_multiplier=64)
            g.affine_select(out=indf[:, :], in_=indf[:, :],
                            compare_op=Alu.is_ge, fill=0.0, base=0,
                            pattern=[[1, 128]], channel_multiplier=-16)
            return g.affine_select(out=indf[:, :], in_=indf[:, :],
                                   compare_op=Alu.is_ge, fill=0.0, base=15,
                                   pattern=[[-1, 128]], channel_multiplier=16)
        em("gpsimd", bmask_sel, waits=[(c_v, CV_BM1)], inc=c_g)
        CG_BM = c_g.n

        ag = cc_ag_in.ap()
        em("sync", lambda e: e.dma_start(
            out=ag[OFF_G:OFF_G + 512 * 1024].rearrange("(m p o) -> p m o", p=P, o=512),
            in_=qwgT), waits=[(c_v, CV_WTR_LAST)], inc=d_cc, inck=16)
        em("sync", lambda e: e.dma_start(
            out=ag[OFF_U:OFF_U + 512 * 1024].rearrange("(m p o) -> p m o", p=P, o=512),
            in_=qwuT), inc=d_cc, inck=16)
        em("sync", lambda e: e.dma_start(
            out=ag[OFF_D:OFF_D + 512 * 1024].rearrange("(m p o) -> p m o", p=P, o=128),
            in_=qwdT), inc=d_cc, inck=16)
        DCC_AG = d_cc.n
        em("gpsimd", lambda e: e.collective_compute(
            "AllGather", Alu.bypass, replica_groups=[list(range(NCORES))],
            ins=[cc_ag_in.ap()], outs=[cc_ag_out.ap()]),
           waits=[(d_cc, DCC_AG)], inc=s_cc)
        SCC_AG = s_cc.n


        for m in range(8):
            em("tensor", lambda e, m=m: e.matmul(
                psT[0:P, 8 * m:8 * m + 8], xsum_all[:, 128 * m:128 * m + 128],
                identf[0:8, 0:8], start=True, stop=True),
               waits=[(d_cb, DCC_BACK), (c_v, CV_WTR_LAST)] if m == 0 else [],
               inc=c_t if m == 7 else None)
        CT_XST = c_t.n
        em("vector", lambda e: e.tensor_copy(xsumT[:, :], psT[:, 0:64]),
           waits=[(c_t, CT_XST)])
        em("vector", lambda e: SEP(e), inc=c_v)
        CV_XST = c_v.n
        for m in range(8):
            em("tensor", lambda e, m=m: e.matmul(
                psA[0:8, 128 * m:128 * m + 128], xsumT[:, 8 * m:8 * m + 8],
                BW[:, 2, :], start=True, stop=True),
               waits=[(c_v, CV_XST)] if m == 0 else [], inc=c_t if m == 7 else None)
        CT_VS = c_t.n
        em("vector", lambda e: e.tensor_copy(vsum_sb[:, :], psA[0:8, :]),
           waits=[(c_t, CT_VS)])
        em("vector", lambda e: SEP(e), inc=c_v)
        CV_VS = c_v.n
        CV_ADD = [0] * NT
        DXP = [0] * NT

        def emit_ln2_stats(jj_waits):
            for jj in range(4):
                wts = list(jj_waits[jj])
                em("vector", lambda e, jj=jj: e.scalar_tensor_tensor(
                    scrV2[:], xp_buf[:, jj, :], 1.0, xp_buf[:, jj, :],
                    Alu.mult, Alu.mult, accum_out=st2_ssq[:, jj:jj + 1]),
                   waits=wts)
                em("vector", lambda e, jj=jj: e.tensor_scalar(
                    scrV2[:], xp_buf[:, jj, :], 1.0, 0.0, Alu.mult, Alu.add,
                    accum_out=st2_sum[:, jj:jj + 1]))

            def ln2math(v):
                SEP(v)
                v.tensor_scalar(st2_a[:], st2_sum[:], 1.0 / E, None, Alu.mult)
                v.tensor_mul(st2_b[:], st2_sum[:], st2_sum[:])
                v.tensor_scalar(st2_rstd[:], st2_ssq[:], 1.0 / E, None, Alu.mult)
                SEP(v)
                v.tensor_scalar(st2_b[:], st2_b[:], 1.0 / (E * E), None, Alu.mult)
                SEP(v)
                v.tensor_sub(st2_rstd[:], st2_rstd[:], st2_b[:])
                return SEP(v)
            em("vector", ln2math, inc=c_v)
            return c_v.n

        vmat = scrV[0:16, :]
        pt_sl = [se2[0:16, 0:128], se2[0:16, 128:256]]
        CV_PTEV = {}
        CT_ATTN = {}
        CV_ADDR = {}
        for j in range(NT):
            c = j // 2
            if j % 2 == 0:
                wts = [(c_v, CV_VS)] if c == 0 else [(c_v, CV_VM)]
                if c == 0:
                    wts.append((c_g, CG_BM))
                for half in range(2):
                    em("tensor", lambda e, c=c, half=half: e.matmul(
                        psA[0:16, 512 * half:512 * half + 512],
                        indf[:, 16 * c:16 * c + 16],
                        vsum_sb[0:8, 512 * half:512 * half + 512],
                        start=True, stop=True),
                       waits=wts if half == 0 else [],
                       inc=c_t if half == 1 else None)
                ct_rep = c_t.n
                em("vector", lambda e: e.tensor_mul(
                    vmat, psA[0:16, :], bmask[:, :]),
                   waits=[(c_t, ct_rep)], inc=c_v)
                CV_VM = c_v.n
            wts = [(c_v, CV_PTEV[j - 4])] if j >= 4 else [(c_v, CV_XST)]
            if j == 0:
                wts.append((c_v, CV_P))
            em("tensor", lambda e, j=j: e.matmul(
                psT[0:16, 128 * (j % 4):128 * (j % 4) + 128],
                st_p[:, 16 * j:16 * j + 16], identf[:], start=True, stop=True),
               waits=wts, inc=c_t)
            ct_pt = c_t.n
            em("vector", lambda e, j=j: e.tensor_copy(
                pt_sl[j % 2], psT[0:16, 128 * (j % 4):128 * (j % 4) + 128]),
               waits=[(c_t, ct_pt)], inc=c_v)
            CV_PTEV[j] = c_v.n
            psX = psB if j % 2 == 0 else psC
            for half in range(2):
                wts = [(c_v, CV_PTEV[j]), (c_v, CV_VM)] if half == 0 else []
                if half == 0 and j >= 2:
                    wts.append((c_v, CV_ADDR[j - 2]))
                em("tensor", lambda e, j=j, half=half, psX=psX: e.matmul(
                    psX[:, 512 * half:512 * half + 512],
                    pt_sl[j % 2], vmat[:, 512 * half:512 * half + 512],
                    start=True, stop=True),
                   waits=wts, inc=c_t if half == 1 else None)
            CT_ATTN[j] = c_t.n
            em("vector", lambda e, j=j, psX=psX: e.tensor_add(
                xr[j][:, :], psX[:, :], xr[j][:, :]),
               waits=[(c_t, CT_ATTN[j])], inc=c_v)
            CV_ADDR[j] = c_v.n
            em("vector", lambda e: SEP(e), inc=c_v)
            CV_ADD[j] = c_v.n
            SYM[("CV_ADD", j)] = (c_v, c_v.n)
            em("sync", lambda e, j=j: e.dma_start(
                out=xprime.ap()[j * P:(j + 1) * P, :], in_=xr[j]),
               waits=[(c_v, CV_ADD[j])], inc=d_xp, inck=16)
            DXP[j] = d_xp.n
            if j == 3:
                CV_LN2_0 = emit_ln2_stats([[(d_cc, DCC_AG)], [], [], []])

        ago = cc_ag_out.ap().rearrange("(r q) -> r q", q=SHARD)

        def fetch_gu(e, ring, r, hb, off):
            src2 = ago[:, off:off + 512 * 1024].rearrange(
                "r (m p o) -> r m p o", p=P, o=512)[r, :, :, 256 * hb:256 * hb + 256]
            return e.dma_start(out=ring, in_=src2.transpose([1, 0, 2]))

        wdr = wd_repack.ap().rearrange("(m p) o -> m p o", p=P)

        def fetch_wd(e, ring, g):
            return e.dma_start(
                out=ring, in_=wdr[2 * g:2 * g + 2].transpose([1, 0, 2]))

        CT_GU = {}
        CV_OUT = {}
        CV_QXEV = {}
        FRONT_EVACS = []
        CT_DONE = [0] * NSUP
        DWS_GU = {}
        DWS_D = {}
        DRS = {}

        agd = ago[:, OFF_D:OFF_D + 512 * 1024].rearrange(
            "r (m p o) -> r m p o", p=P, o=128)

        def emit_repack():
            DWS_LD = {}
            DWS_ST = {}
            for r in range(8):
                wts = []
                if r == 0:
                    wts = [(s_cc, SCC_AG)]
                elif r >= 2:
                    wts = [(d_ws, DWS_ST[r - 2])]
                em("sync", lambda e, r=r: e.dma_start(
                    out=xstage[r % 2], in_=agd[r].transpose([1, 0, 2])),
                   waits=wts, inc=d_ws, inck=16)
                DWS_LD[r] = d_ws.n
                em("sync", lambda e, r=r: e.dma_start(
                    out=wdr[:, :, 128 * r:128 * r + 128].transpose([1, 0, 2]),
                    in_=xstage[r % 2]), waits=[(d_ws, DWS_LD[r])],
                   inc=d_ws, inck=16)
                DWS_ST[r] = d_ws.n
            SYM[("DWS_REP", 0)] = (d_ws, d_ws.n)
            return d_ws.n

        DXL_S = {}

        def front_dma(sg, pre_repack=False):
            if sg > 0:
                for t in range(4):
                    wts = [("CS_QH", (sg - 1, 11))] if t == 0 else []
                    em("sync", lambda e, sg=sg, t=t: e.dma_start(
                        out=xp_buf[:, t, :],
                        in_=xprime.ap()[512 * sg + 128 * t:512 * sg + 128 * (t + 1), :]),
                       waits=wts, inc=d_xl, inck=16)
                    DXL_S[(sg, t)] = d_xl.n

            for fi in range(16):
                r, hb = fi // 2, fi % 2
                wts = [(s_cc, SCC_AG)] if (sg == 0 and fi == 0) else []
                if sg > 0 or fi >= 3:
                    pr, pn = (sg, 2 * (fi - 3) + 1) if fi >= 3 \
                        else (sg - 1, [31, 27, 29][fi])
                    wts.append(("CT_GU", (pr, pn)))
                em("sync", lambda e, r=r, hb=hb, fi=fi: fetch_gu(
                    e, wg_ring[fi % 3], r, hb, OFF_G), waits=wts,
                   inc=d_g[fi % 2], inck=16)
                em("sync", lambda e, r=r, hb=hb, fi=fi: fetch_gu(
                    e, wu_ring[fi % 3], r, hb, OFF_U), inc=d_g[fi % 2], inck=16)
                DWS_GU[(sg, fi)] = (d_g[fi % 2], d_g[fi % 2].n)
                if pre_repack and fi == 2:
                    emit_repack()
            for g in range(16):
                wts = []
                gi = 16 * sg + g
                if sg == 1 and g == 0:
                    wts.append(("DWS_REP", 0))
                if gi >= 3:
                    pg = gi - 3
                    wts.append(("CT_DW", (pg // 16, pg % 16)))
                if sg == 0:
                    for s in range(8):
                        em("gpsimd", lambda e, g=g, s=s, gi=gi: e.dma_start(
                            out=wd_ring[gi % 3][:, :, 128 * s:128 * s + 128],
                            in_=agd[s][2 * g:2 * g + 2].transpose([1, 0, 2])),
                           waits=(wts + [(s_cc, SCC_AG)]) if s == 0 else [],
                           inc=d_d[gi % 3], inck=16)
                else:
                    em("gpsimd", lambda e, sg=sg, g=g, gi=gi: fetch_wd(
                        e, wd_ring[gi % 3], g), waits=wts, inc=d_d[gi % 3], inck=16)
                DWS_D[(sg, g)] = (d_d[gi % 3], d_d[gi % 3].n)
            for b in range(4):
                wts = [(d_xp, 16 * (4 * sg + b + 1))]
                prev = 4 * sg + b - 1
                if prev >= 1:
                    wts.append((d_y, 16 * prev))
                em("gpsimd", lambda e, sg=sg, b=b: e.dma_start(
                    out=res_ring[b % 2],
                    in_=xprime.ap()[512 * sg + 128 * b:512 * sg + 128 * (b + 1), :]),
                   waits=wts, inc=d_rs, inck=16)
                DRS[(sg, b)] = d_rs.n

        CV_LN2S = {0: None}

        def front_stats(sg):
            CV_LN2S[sg] = emit_ln2_stats(
                [[(d_xl, DXL_S[(sg, t)])] for t in range(4)])

        def front_rest(sg):
            if sg > 0 and sg not in CV_LN2S:
                front_stats(sg)
            CV_LN2 = CV_LN2_0 if sg == 0 else CV_LN2S[sg]

            def ln2rstd(s):
                s.activation(st2_b[:], st2_rstd[:], Act.Ln, bias=cst[:, 0:1],
                             scale=cst[:, 4:5])
                return s.activation(st2_rstd[:], st2_b[:], Act.Exp, bias=cst[:, 5:6],
                                    scale=cst[:, 2:3])
            em("scalar", ln2rstd, waits=[(c_v, CV_LN2)], inc=c_s)
            CS_R2 = c_s.n
            em("vector", lambda e: e.scalar_tensor_tensor(
                st2_lnb[:], st2_a[:], -1.0, st2_rstd[:], Alu.mult, Alu.mult),
               waits=[(c_s, CS_R2)])
            em("vector", lambda e: SEP(e), inc=c_v)
            CV_LNB2 = c_v.n

            def chain(v, k):
                slot = x2s[k % 4]
                v.tensor_scalar(st_c[:, k:k + 1], st_mx2[:, k:k + 1], EPS, None, Alu.max)
                SEP(v)
                v.reciprocal(st_rc[:, k:k + 1], st_c[:, k:k + 1])
                v.tensor_mul(st_c3a[:, k:k + 1], st_c[:, k:k + 1], st_c[:, k:k + 1])
                v.tensor_scalar(scrV2[:], slot, 127.0, None, Alu.mult)
                v.tensor_mul(st_s0[:, k:k + 1], st_c3a[:, k:k + 1], st_c[:, k:k + 1])
                v.tensor_scalar(scrV2[:], scrV2[:], st_rc[:, k:k + 1], MAGIC,
                                Alu.mult, Alu.add)
                v.tensor_scalar(st_F[:, k:k + 1], st_s0[:, k:k + 1],
                                kconsts[:, 3:4], None, Alu.mult)
                return v.tensor_scalar(qx_slot(k), scrV2[:], MAGIC, None, Alu.subtract)

            def emit_qxT(k):
                wts = [("CV_QX", (sg, k))]
                if k == 0 and sg > 0:
                    wts.append(("CV_OUT0", sg - 1))
                if k > 0:
                    wts.append(("CV_QXEV", (sg, k - 1)))
                for m in range(8):
                    em("tensor", lambda e, k=k, m=m: e.matmul(
                        psT[:, 128 * m:128 * m + 128],
                        qx_slot(k)[:, 128 * m:128 * m + 128],
                        identb[:], start=True, stop=True),
                       waits=wts if m == 0 else [], inc=c_t if m == 7 else None)
                SYM[("CT_QXT", sg, k)] = (c_t, c_t.n)
                ctv = c_t.n

                def do_evac():
                    em("vector", lambda e, k=k: e.tensor_copy(
                        qxT[:, :, 128 * k:128 * k + 128],
                        psT[:, :].rearrange("p (m o) -> p m o", o=128)),
                       waits=[(c_t, ctv)], inc=c_v)
                    CV_QXEV[(sg, k)] = c_v.n
                    SYM[("CV_QXEV", sg, k)] = (c_v, c_v.n)
                if sg == 0:
                    do_evac()
                else:
                    FRONT_EVACS.append(do_evac)

            for jj in range(4):
                wts = [(c_v, CV_LNB2)]
                if sg >= 1:
                    wts.append(("CT_QXT", (sg - 1, jj)))
                em("scalar", lambda e, jj=jj: e.activation(
                    x2s[jj % 4], xp_buf[:, jj, :], Act.Identity,
                    scale=st2_rstd[:, jj:jj + 1], bias=st2_lnb[:, jj:jj + 1]),
                   waits=wts, inc=c_s)
                cs_x2 = c_s.n
                em("vector", lambda e, jj=jj: e.reduce_max(
                    st_mx2[:, jj:jj + 1], x2s[jj % 4], AX, apply_absolute_value=True),
                   waits=[(c_s, cs_x2)])
                if jj >= 1:
                    em("vector", lambda e, k=jj - 1: chain(e, k), inc=c_v)
                    SYM[("CV_QX", sg, jj - 1)] = (c_v, c_v.n)
                    emit_qxT(jj - 1)
            em("vector", lambda e: chain(e, 3), inc=c_v)
            SYM[("CV_QX", sg, 3)] = (c_v, c_v.n)
            emit_qxT(3)

        def emit_front(sg, pre_repack=False):
            front_dma(sg, pre_repack)
            if sg > 0:
                front_stats(sg)
            front_rest(sg)

        def emit_mid(sg):
            for n in range(32):
                fi, ol = n // 2, n % 2
                gbank = psA[:, 0:512] if n % 2 == 0 else psB[:, 0:512]
                ubank = psA[:, 512:1024] if n % 2 == 0 else psB[:, 512:1024]
                wts = [DWS_GU[(sg, fi)]]
                if n == 0:
                    wts.append((c_v, CV_QXEV[(sg, 3)]))
                if n == 12 and sg > 0:
                    wts.append((d_y, 64 * sg))
                if n >= 2:
                    wts += [("CS_RG", (sg, n - 2)), ("CV_A", (sg, n - 2))]
                elif sg > 0:
                    wts.append((c_v, CV_OUT[(sg - 1, 3)]))
                for a in range(8):
                    em("tensor", lambda e, a=a, fi=fi, ol=ol, gbank=gbank: e.matmul(
                        gbank, wg_ring[fi % 3][:, a, 128 * ol:128 * ol + 128],
                        qxT[:, a, :], start=(a == 0), stop=(a == 7)),
                       waits=wts if a == 0 else [])
                for a in range(8):
                    em("tensor", lambda e, a=a, fi=fi, ol=ol, ubank=ubank: e.matmul(
                        ubank, wu_ring[fi % 3][:, a, 128 * ol:128 * ol + 128],
                        qxT[:, a, :], start=(a == 0), stop=(a == 7)),
                       inc=c_t if a == 7 else None)
                CT_GU[(sg, n)] = c_t.n
                SYM[("CT_GU", sg, n)] = (c_t, c_t.n)

                em("scalar", lambda e, n=n, gbank=gbank: e.activation(
                    rg_sb[n % 2][:], gbank, Act.Relu),
                   waits=[(c_t, CT_GU[(sg, n)])], inc=c_s)
                SYM[("CS_RG", sg, n)] = (c_s, c_s.n)
                em("vector", lambda e, n=n, ubank=ubank: e.scalar_tensor_tensor(
                    A_sb[:], ubank, 1.0, rg_sb[n % 2][:], Alu.mult, Alu.mult),
                   waits=[("CS_RG", (sg, n))], inc=c_v)
                SYM[("CV_A", sg, n)] = (c_v, c_v.n)
                hw = [(d_xp, 16 * (n // 2 + 1))] if sg == 0 else []
                em("vector", lambda e, n=n: e.tensor_mul(h_sb[:, n, :], rg_sb[n % 2][:],
                                                         A_sb[:]), waits=hw, inc=c_v)
                if n == 0:
                    em("vector", lambda e: e.tensor_scalar(
                        mxh[:], h_sb[:, 0, :], -1.0, None, Alu.mult))
                else:
                    em("vector", lambda e, n=n: e.scalar_tensor_tensor(
                        mxh[:], h_sb[:, n, :], -1.0, mxh[:], Alu.mult, Alu.max))
                em("vector", lambda e, n=n: e.tensor_tensor(
                    mxh[:], mxh[:], h_sb[:, n, :], Alu.max), inc=c_v)
                if n == 31:
                    SYM[("CV_MIDV", sg)] = (c_v, c_v.n)

        def emit_tok(sg):
            em("vector", lambda e: SEP(e), inc=c_v)
            CV_HLAST = c_v.n
            for b in range(4):
                em("tensor", lambda e, b=b: e.matmul(
                    psT[:, 128 * b:128 * b + 128], mxh[:, 128 * b:128 * b + 128],
                    identf[:], start=True, stop=True),
                   waits=[(c_v, CV_HLAST)] if b == 0 else [],
                   inc=c_t if b == 3 else None)
            CT_MXT = c_t.n

            def tokmath(v):
                for b in range(4):
                    v.reduce_max(st_mxh[:, b:b + 1], psT[:, 128 * b:128 * b + 128], AX)
                SEP(v)
                v.tensor_mul(st_m1[:], st_mxh[:], st_F[:])
                SEP(v)
                v.tensor_scalar(st_m2[:], st_m1[:], EPS, None, Alu.max)
                SEP(v)
                v.reciprocal(st_rc2[:], st_m2[:])
                v.tensor_scalar(st_fd[:], st_m2[:], kconsts[:, 4:5], None, Alu.mult)
                SEP(v)
                v.tensor_mul(st_s0[:], st_rc2[:], st_F[:])
                SEP(v)
                v.tensor_scalar(st_sint[:], st_s0[:], 127.0, None, Alu.mult)
                return SEP(v)
            em("vector", tokmath, waits=[(c_t, CT_MXT)], inc=c_v)
            CV_TOK = c_v.n
            for b in range(4):
                em("tensor", lambda e, b=b: e.matmul(
                    psT[0:1, 128 * b:128 * b + 128], st_sint[:, b:b + 1], identf[:],
                    start=True, stop=True),
                   waits=[(c_v, CV_TOK)] if b == 0 else [], inc=c_t if b == 3 else None)
            CT_ST = c_t.n
            em("vector", lambda e: e.tensor_copy(sseg_row[:, :], psT[0:1, 0:512]),
               waits=[(c_t, CT_ST)])
            em("vector", lambda e: SEP(e), inc=c_v)
            CV_SSEG = c_v.n
            em("tensor", lambda e: e.matmul(
                psT[0:P, 0:512], onesf1[0:1, :], sseg_row[0:1, :],
                start=True, stop=True),
               waits=[(c_v, CV_SSEG)], inc=c_t)
            CT_SBB = c_t.n
            em("vector", lambda e: e.tensor_copy(s_b[:], psT[:, 0:512]),
               waits=[(c_t, CT_SBB)], inc=c_v)
            return c_v.n

        def emit_qh(sg, CV_SBV, v_hook=None):
            CS_QH = {}
            for n in range(32):
                if n == 18 and v_hook is not None:
                    v_hook()
                wts = [(c_v, CV_SBV)]
                if n >= 2:
                    wts.append(("CS_QH", (sg, n - 2)))
                em("vector", lambda e, n=n: e.tensor_mul(qh0[n % 2][:], h_sb[:, n, :],
                                                         s_b[:]),
                   waits=wts, inc=c_v)
                cv_q0 = c_v.n
                em("scalar", lambda e, n=n: e.activation(
                    scrS3[:], qh0[n % 2][:], Act.Identity, bias=cst[:, 1:2],
                    scale=cst[:, 4:5]),
                   waits=[(c_v, cv_q0)])
                qw = []
                if n >= 8:
                    qw.append(("CT_DW", (sg, (n - 8) // 2)))
                elif sg >= 1:
                    qw.append(("CT_DW", (sg - 1, (24 + n) // 2)))
                em("scalar", lambda e, n=n: e.activation(
                    qh_sb[:, n % 8, :], scrS3[:], Act.Identity, bias=cst[:, 6:7],
                    scale=cst[:, 4:5]), waits=qw, inc=c_s)
                CS_QH[n] = c_s.n
                SYM[("CS_QH", sg, n)] = (c_s, c_s.n)
            return CS_QH

        def emit_down(sg):
            bank_of = [psT[:, 0:512], psT[:, 512:1024], psC[:, 0:512], psC[:, 512:1024],
                       psA[:, 0:512], psA[:, 512:1024], psB[:, 0:512], psB[:, 512:1024]]
            for i2 in range(32):
                g = i2 // 2
                wts = [("CS_QH", (sg, i2))]
                if i2 == 0:
                    wts.append(DWS_D[(sg, 0)])
                    if sg > 0:
                        wts.append((c_v, CV_OUT[(sg - 1, 3)]))
                elif i2 % 2 == 0:
                    wts.append(DWS_D[(sg, g)])
                gi = 16 * sg + g
                for b in range(4):
                    for half in range(2):
                        em("tensor", lambda e, i2=i2, b=b, half=half, gi=gi: e.matmul(
                            bank_of[2 * b + half],
                            qh_sb[:, i2 % 8, 128 * b:128 * b + 128],
                            wd_ring[gi % 3][:, i2 % 2, 512 * half:512 * half + 512],
                            start=(i2 == 0), stop=(i2 == 31)),
                           waits=wts if (b == 0 and half == 0) else [],
                           inc=(c_t if (i2 % 2 == 1 and b == 3 and half == 1) else None))
                if i2 % 2 == 1:
                    SYM[("CT_DW", sg, g)] = (c_t, c_t.n)
            CT_DONE[sg] = c_t.n
            SYM[("CT_DONE", sg)] = (c_t, c_t.n)
            return bank_of

        def emit_evac(sg, bank_of):
            for b in range(4):
                def devac(v, sg=sg, b=b):
                    v.scalar_tensor_tensor(out_sb[b % 4][:, 0:512], bank_of[2 * b],
                                           st_fd[:, b:b + 1], res_ring[b % 2][:, 0:512],
                                           Alu.mult, Alu.add)
                    return v.scalar_tensor_tensor(out_sb[b % 4][:, 512:1024],
                                                  bank_of[2 * b + 1],
                                                  st_fd[:, b:b + 1],
                                                  res_ring[b % 2][:, 512:1024],
                                                  Alu.mult, Alu.add)
                wts = [(c_t, CT_DONE[sg]), (d_rs, DRS[(sg, b)])]
                if sg > 0:
                    wts.append((d_y, 16 * (4 * sg + b - 3)))
                em("vector", devac, waits=wts)
                em("vector", lambda e: SEP(e), inc=c_v)
                CV_OUT[(sg, b)] = c_v.n
                if b == 0:
                    SYM[("CV_OUT0", sg)] = (c_v, c_v.n)
                if 1 <= b < len(FRONT_EVACS) + 1:
                    FRONT_EVACS[b - 1]()
            em("vector", lambda e: SEP(e), inc=c_v)
            cv_tail = c_v.n
            if len(FRONT_EVACS) == 4:
                FRONT_EVACS[3]()
            FRONT_EVACS.clear()
            for b in range(4):
                j = 4 * sg + b
                em("scalar", lambda e, j=j, b=b: e.dma_start(
                    out=y_out[j * P:(j + 1) * P, :], in_=out_sb[b % 4][:]),
                   waits=[(c_v, CV_OUT[(sg, b + 1)] if b < 3 else cv_tail)],
                   inc=d_y, inck=16)

        emit_front(0)
        for sg in range(NSUP):
            emit_mid(sg)
            cv_sbv = emit_tok(sg)
            nxt = sg + 1
            if nxt < NSUP:
                front_dma(nxt, pre_repack=(sg == 0))
                emit_qh(sg, cv_sbv, v_hook=lambda nxt=nxt: front_stats(nxt))
                bank_of = emit_down(sg)
                front_rest(nxt)
            else:
                emit_qh(sg, cv_sbv)
                bank_of = emit_down(sg)
            emit_evac(sg, bank_of)

        em("gpsimd", lambda e: e.engine_nop(), waits=[(d_y, 16 * NT)])

        resolve_all()

        with nc.Block() as block:
            @block.sync
            def _(e):
                W.run("sync", e)

            @block.gpsimd
            def _(e):
                W.run("gpsimd", e)

            @block.tensor
            def _(e):
                W.run("tensor", e)

            @block.vector
            def _(e):
                W.run("vector", e)

            @block.scalar
            def _(e):
                W.run("scalar", e)

    return nc


_NC_CACHE = None


def kernel(L0_attn_x, L0_attn_Wq, L0_attn_Wk, L0_attn_Wv,
           L0_ffn_w_gate, L0_ffn_w_up, L0_ffn_w_down):
    global _NC_CACHE
    x = np.ascontiguousarray(np.asarray(L0_attn_x, dtype=np.float32))
    wq = np.ascontiguousarray(np.asarray(L0_attn_Wq, dtype=np.float32))
    wk = np.ascontiguousarray(np.asarray(L0_attn_Wk, dtype=np.float32))
    wv = np.ascontiguousarray(np.asarray(L0_attn_Wv, dtype=np.float32))
    wg = np.ascontiguousarray(np.asarray(L0_ffn_w_gate, dtype=np.float32))
    wu = np.ascontiguousarray(np.asarray(L0_ffn_w_up, dtype=np.float32))
    wd = np.ascontiguousarray(np.asarray(L0_ffn_w_down, dtype=np.float32))

    in_maps = []
    for i in range(NCORES):
        xi = np.ascontiguousarray(x[:, TPC * i:TPC * (i + 1), :]).reshape(R, E)
        in_maps.append({
            "x": xi, "wq": wq, "wk": wk, "wv": wv,
            "wg_sh": np.ascontiguousarray(wg[512 * i:512 * (i + 1)]),
            "wu_sh": np.ascontiguousarray(wu[512 * i:512 * (i + 1)]),
            "wd_sh": np.ascontiguousarray(wd[128 * i:128 * (i + 1)]),
        })
    if _NC_CACHE is None:
        _NC_CACHE = build_nc()
    res = run_bass_kernel_spmd(_NC_CACHE, in_maps, core_ids=list(range(NCORES)))
    out = np.empty((C, 2048, E), dtype=np.float32)
    for i in range(NCORES):
        out[:, TPC * i:TPC * (i + 1), :] = res.results[i]["y"].reshape(C, TPC, E)
    return out
